# version 02 (final)
# baseline (speedup 1.0000x reference)
import numpy as np
import ml_dtypes
import concourse.bass as bass
import concourse.mybir as mybir
from concourse.bass_utils import run_bass_kernel_spmd

F32 = mybir.dt.float32
BF16 = mybir.dt.bfloat16
AF = mybir.ActivationFunctionType
ALU = mybir.AluOpType
AX = mybir.AxisListType

NCORES = 8
BB, TT, DD = 2, 8192, 1024
PP = TT // NCORES
COLS = BB * PP

ENGINES = ("tensor", "vector", "scalar", "gpsimd", "sync")
DMA_POOL = {"sync": 20, "gpsimd": 10, "scalar": 6, "vector": 2, "tensor": 1}


class Op:
    __slots__ = ("engine", "fn", "reads", "writes", "dma", "idx", "inc_sem",
                 "inc_val", "milestone", "deps")

    def __init__(self, engine, fn, reads, writes, dma):
        self.engine = engine
        self.fn = fn
        self.reads = reads
        self.writes = writes
        self.dma = dma
        self.idx = -1
        self.inc_sem = None
        self.inc_val = 0
        self.milestone = None
        self.deps = set()


class Em:
    def __init__(self, nc):
        self.nc = nc
        self.ops = []
        self.last_writer = {}
        self.readers = {}

    def add(self, engine, fn, reads=(), writes=(), dma=False):
        assert engine in ENGINES, engine
        op = Op(engine, fn, tuple(reads), tuple(writes), dma)
        op.idx = len(self.ops)
        for k in op.reads:
            w = self.last_writer.get(k)
            if w is not None:
                op.deps.add(w)
            self.readers.setdefault(k, []).append(op.idx)
        for k in op.writes:
            w = self.last_writer.get(k)
            if w is not None:
                op.deps.add(w)
            for r in self.readers.get(k, ()):
                op.deps.add(r)
            self.readers[k] = []
            self.last_writer[k] = op.idx
        op.deps.discard(op.idx)
        self.ops.append(op)
        return op

    def dma(self, queue, out, in_, reads=(), writes=(), **kw):
        return self.add(queue, lambda e: e.dma_start(out=out, in_=in_, **kw),
                        reads=reads, writes=writes, dma=True)

    SERIALIZE = False

    def finalize(self):
        nc = self.nc
        ops = self.ops
        if Em.SERIALIZE:
            for op in ops:
                if op.idx > 0:
                    op.deps.add(op.idx - 1)
        needed = set()
        for op in ops:
            for d in op.deps:
                needed.add(d)

        sem_keys = []
        for e in ENGINES:
            if any(o.engine == e and not o.dma for o in ops):
                sem_keys.append(("c", e))
        dma_slot_count = {}
        for e in ENGINES:
            n = sum(1 for o in ops if o.engine == e and o.dma)
            if n:
                k = min(DMA_POOL[e], n)
                dma_slot_count[e] = k
                for i in range(k):
                    sem_keys.append(("d", e, i))

        comp_counter = {e: 0 for e in ENGINES}
        dma_rot = {e: 0 for e in ENGINES}
        dma_uses = {}
        extra_waits = {}
        for op in ops:
            if op.dma:
                e = op.engine
                slot = dma_rot[e] % dma_slot_count[e]
                dma_rot[e] += 1
                key = ("d", e, slot)
                prev = dma_uses.get(key, 0)
                if prev:
                    extra_waits.setdefault(op.idx, []).append((key, 16 * prev))
                dma_uses[key] = prev + 1
                op.inc_sem = key
                op.inc_val = 16
                op.milestone = (key, 16 * (prev + 1))
            elif op.idx in needed:
                e = op.engine
                comp_counter[e] += 1
                key = ("c", e)
                op.inc_sem = key
                op.inc_val = 1
                op.milestone = (key, comp_counter[e])

        guar = {e: {} for e in ENGINES}
        waits = []
        for op in ops:
            g = guar[op.engine]
            want = {}
            for sk, v in extra_waits.get(op.idx, ()):
                if want.get(sk, 0) < v:
                    want[sk] = v
            for d in op.deps:
                m = ops[d].milestone
                assert m is not None
                sk, v = m
                if want.get(sk, 0) < v:
                    want[sk] = v
            wl = []
            for sk, v in want.items():
                if g.get(sk, 0) >= v:
                    continue
                wl.append((sk, v))
                g[sk] = v
            waits.append(wl)
        self.debug_waits = waits
        self.debug_ms = [op.milestone for op in ops]
        if True:
            pass

        sems = {}
        ctxs = []
        for sk in sem_keys:
            cm = nc.semaphore("s_" + "_".join(map(str, sk)))
            sems[sk] = cm.__enter__()
            ctxs.append(cm)

        final_targets = [(k, v) for k, v in
                         [(("c", e), comp_counter[e]) for e in ENGINES if comp_counter[e] > 0]
                         + [(k, 16 * u) for k, u in dma_uses.items()]]

        block_cm = nc.Block()
        block = block_cm.__enter__()

        def make(engine_name):
            def emitter(eng):
                for op, wl in zip(ops, waits):
                    if op.engine != engine_name:
                        continue
                    for sk, v in wl:
                        eng.wait_ge(sems[sk], v)
                    inst = op.fn(eng)
                    if op.inc_val:
                        assert inst is not None, f"op {op.idx} returned no inst"
                        inst.then_inc(sems[op.inc_sem], op.inc_val)
                if engine_name == "sync":
                    for sk, v in final_targets:
                        eng.wait_ge(sems[sk], v)

            return emitter

        used_engines = {o.engine for o in ops} | {"sync"}
        for e in ENGINES:
            if e in used_engines:
                getattr(block, e)(make(e))
        block_cm.__exit__(None, None, None)
        for cm in ctxs:
            cm.__exit__(None, None, None)




def chunks_of(total, size):
    out = []
    a = 0
    while a < total:
        out.append((a, min(size, total - a)))
        a += size
    return out


def fm_host(x, q=128):
    a, p, d = x.shape
    return np.ascontiguousarray(x.reshape(a, p, d // q, q).transpose(2, 3, 0, 1))


def fm_unhost(x):
    c, q, a, p = x.shape
    return np.ascontiguousarray(x.transpose(2, 3, 0, 1).reshape(a, p, c * q))


class Stage:
    CONST_CAP = 64

    def __init__(self, name):
        self.nc = bass.Bass("TRN2", target_bir_lowering=False)
        self.em = Em(self.nc)
        self.name = name
        self.in_specs = {}
        self.out_names = []
        self._ctx = []
        self.const_vals = []
        self._const_tile = None

    def const(self, v):
        v = float(v)
        if self._const_tile is None:
            cap = Stage.CONST_CAP
            inp = self.inp("consts", (128, cap))
            self._const_tile = self.sb("CONSTS", [128, cap])
            self.em.dma("sync", self._const_tile[:, :], inp, writes=("consts",))
        if v in self.const_vals:
            j = self.const_vals.index(v)
        else:
            self.const_vals.append(v)
            j = len(self.const_vals) - 1
            assert j < Stage.CONST_CAP
        return self._const_tile[:, j:j + 1]

    def const_host(self):
        arr = np.zeros((128, Stage.CONST_CAP), np.float32)
        for j, v in enumerate(self.const_vals):
            arr[:, j] = v
        return arr

    def inp(self, name, shape, dtype=F32):
        self.in_specs[name] = (tuple(shape), dtype)
        return self.nc.dram_tensor(name, list(shape), dtype,
                                   kind="ExternalInput").ap()

    def out(self, name, shape, dtype=F32):
        self.out_names.append(name)
        return self.nc.dram_tensor(name, list(shape), dtype,
                                   kind="ExternalOutput").ap()

    def sb(self, name, shape, dtype=F32):
        cm = self.nc.sbuf_tensor(name, list(shape), dtype)
        t = cm.__enter__()
        self._ctx.append(cm)
        return t

    def ps(self, name, shape, dtype=F32):
        cm = self.nc.psum_tensor(name, list(shape), dtype)
        t = cm.__enter__()
        self._ctx.append(cm)
        return t

    def finish(self):
        self.em.finalize()
        for cm in reversed(self._ctx):
            cm.__exit__(None, None, None)
        return self.nc


def e_act(st, out_ap, in_ap, func, bias=0.0, scale=1.0, reads=(), writes=(),
          accum_out=None):
    st.em.add("scalar",
              lambda e: e.activation(out_ap, in_ap, func, bias=bias, scale=scale,
                                     accum_out=accum_out),
              reads=reads, writes=writes)


def e_tt(st, out_ap, a_ap, b_ap, op, reads=(), writes=(), eng="vector"):
    st.em.add(eng, lambda e: e.tensor_tensor(out_ap, a_ap, b_ap, op),
              reads=reads, writes=writes)


def e_ts(st, out_ap, in_ap, s1, s2, op0, op1=None, reads=(), writes=(),
         eng="vector"):
    if op1 is None:
        st.em.add(eng, lambda e: e.tensor_scalar(out_ap, in_ap, s1, None, op0),
                  reads=reads, writes=writes)
    else:
        st.em.add(eng, lambda e: e.tensor_scalar(out_ap, in_ap, s1, s2, op0, op1),
                  reads=reads, writes=writes)


def e_stt(st, out_ap, in0_ap, scalar, in1_ap, op0, op1, reads=(), writes=(),
          eng="vector"):
    st.em.add(eng,
              lambda e: e.scalar_tensor_tensor(out_ap, in0_ap, scalar, in1_ap, op0, op1),
              reads=reads, writes=writes)


def e_copy(st, out_ap, in_ap, reads=(), writes=(), eng="vector"):
    if eng == "scalar":
        st.em.add(eng, lambda e: e.copy(out_ap, in_ap),
                  reads=reads, writes=writes)
    else:
        st.em.add(eng, lambda e: e.tensor_copy(out_ap, in_ap),
                  reads=reads, writes=writes)


def emit_ln(st, x_tiles, xn_tiles, W, nfeat, ones_f, sq, ps_s, ps_q,
            mstat, rstd, eps, keys_x, keys_o, keys_sq, keys_ps, key_m, key_r,
            g_aps=None, b_aps=None, key_gb=None, ones_sq=None):
    em = st.em
    nd = len(x_tiles)
    ones_sq = ones_sq if ones_sq is not None else ones_f

    def mmsum(e):
        inst = None
        for i in range(nd):
            inst = e.matmul(ps_s, ones_f, x_tiles[i], start=(i == 0),
                            stop=(i == nd - 1))
        return inst

    em.add("tensor", mmsum, reads=tuple(keys_x) + ("ones_f",),
           writes=(keys_ps[0],))
    for i in range(nd):
        e_act(st, sq[i % 2], x_tiles[i], AF.Square, reads=(keys_x[i],),
              writes=(keys_sq[i % 2],))
        em.add("tensor",
               lambda e, i=i: e.matmul(ps_q, ones_sq, sq[i % 2], start=(i == 0),
                                       stop=(i == nd - 1)),
               reads=(keys_sq[i % 2], "ones_f"), writes=(keys_ps[1],))
    inv = 1.0 / nfeat
    e_ts(st, mstat, ps_s, inv, None, ALU.mult, reads=(keys_ps[0],),
         writes=(key_m,))
    e_act(st, rstd, mstat, AF.Square, reads=(key_m,), writes=(key_r,))
    e_stt(st, rstd, ps_q, inv, rstd, ALU.mult, ALU.subtract,
          reads=(keys_ps[1], key_r), writes=(key_r,))
    e_act(st, rstd, rstd, AF.Sqrt, bias=st.const(eps),
          reads=(key_r, "consts"), writes=(key_r,))
    st.em.add("vector", lambda e: e.reciprocal(rstd, rstd),
              reads=(key_r,), writes=(key_r,))
    for i in range(nd):
        e_tt(st, xn_tiles[i], x_tiles[i], mstat, ALU.subtract,
             reads=(keys_x[i], key_m), writes=(keys_o[i],))
        if g_aps is None:
            e_tt(st, xn_tiles[i], xn_tiles[i], rstd, ALU.mult,
                 reads=(keys_o[i], key_r), writes=(keys_o[i],))
        else:
            e_tt(st, xn_tiles[i], xn_tiles[i], rstd, ALU.mult,
                 reads=(keys_o[i], key_r), writes=(keys_o[i],))
            bw = list(b_aps[i].shape)[-1]
            if bw == 1:
                e_ts(st, xn_tiles[i], xn_tiles[i], g_aps[i], b_aps[i], ALU.mult,
                     ALU.add, reads=(keys_o[i], key_gb), writes=(keys_o[i],))
            else:
                e_stt(st, xn_tiles[i], xn_tiles[i], g_aps[i], b_aps[i], ALU.mult,
                      ALU.add, reads=(keys_o[i], key_gb), writes=(keys_o[i],))


NHC = 2 * 1027
NRC = 2 * 1024


def real_chunks(csz):
    out = []
    for a in range(2):
        for off, w in chunks_of(1024, csz):
            out.append((a * 1027 + 3 + off, w, a * 1024 + off))
    return out


def build_l0a():
    st = Stage("l0a")
    em = st.em
    xh = st.inp("xh", (8, 128, NHC))
    wfcT = st.inp("wfcT", (1024, 4096))
    wprojT = st.inp("wprojT", (4096, 1024))
    bln = st.inp("bln", (128, 64))
    ones_f = st.inp("ones_f", (128, 128))
    ones_b = st.inp("ones_b", (128, 128), BF16)
    o_xn = st.out("o_xn", (8, 128, NHC))
    o_bit = st.out("o_bit", (8, 128, NRC))
    hln = st.nc.dram_tensor("hln", [128, 32, NRC], BF16).ap()

    XN = st.sb("XN", [128, 8 * NHC])
    WQ = st.sb("WQ", [128, 32768], BF16)
    H = st.sb("H", [128, 32 * 256], BF16)
    QW = st.sb("QW", [128, 2048])
    QSB = st.sb("QSB", [128, 2 * 2048], BF16)
    RH = st.sb("RH", [128, 8 * 256], BF16)
    ONEF = st.sb("ONEF", [128, 128])
    ONEB = st.sb("ONEB", [128, 128], BF16)
    MST = st.sb("MST", [128, 512])
    RST = st.sb("RST", [128, 512])
    SQ = st.sb("SQ", [128, 1024])
    PRT = st.sb("PRT", [128, 48])
    THR = st.sb("THR", [128, 2])
    BLN = st.sb("BLN", [128, 64])
    BO = st.sb("BO", [128, 1024])
    PSA = st.ps("PSA", [128, 512])
    PSB = st.ps("PSB", [128, 512])
    PSC = st.ps("PSC", [128, 512])
    PSD = st.ps("PSD", [128, 512])

    em.dma("sync", ONEF[:, :], ones_f, writes=("onef",))
    em.dma("sync", ONEB[:, :], ones_b, writes=("oneb",))
    em.dma("sync", BLN[:, :], bln, writes=("bln",))
    for i in range(8):
        em.dma("sync", XN[:, i * NHC:(i + 1) * NHC], xh[i], writes=(f"XN{i}",))

    def gamma_pass(widx, tiles, prt_off, thr_col, nelem):
        for j, (dap, tw) in enumerate(tiles):
            em.dma("gpsimd", QW[:, 0:tw], dap, writes=("QW",))
            em.add("scalar",
                   lambda e, tw=tw, j=j: e.activation(
                       QSB[:, 0:tw], QW[:, 0:tw], AF.Abs,
                       accum_out=PRT[:, prt_off + j: prt_off + j + 1]),
                   reads=("QW",), writes=("QSB", f"prt{widx}_{j}"))
        em.add("vector",
               lambda e: e.reduce_sum(THR[:, thr_col:thr_col + 1],
                                      PRT[:, prt_off:prt_off + len(tiles)], AX.X),
               reads=tuple(f"prt{widx}_{j}" for j in range(len(tiles))),
               writes=(f"thrp{widx}",))
        em.add("tensor",
               lambda e: e.matmul(PSA[:, 0:1], ONEF[:, :],
                                  THR[:, thr_col:thr_col + 1], start=True,
                                  stop=True),
               reads=(f"thrp{widx}", "onef"), writes=("PSA",))
        e_ts(st, THR[:, thr_col:thr_col + 1], PSA[:, 0:1],
             0.5 / nelem, 0.5e-6, ALU.mult, ALU.add,
             reads=("PSA",), writes=(f"thr{widx}",))

    w1_tiles = [(wfcT[r * 128:(r + 1) * 128, h * 2048:(h + 1) * 2048], 2048)
                for r in range(8) for h in range(2)]
    w2_tiles = [(wprojT[r * 128:(r + 1) * 128, :], 1024) for r in range(32)]
    gamma_pass(1, w1_tiles, 0, 0, 1024 * 4096)

    for off, w in chunks_of(NHC, 512):
        emit_ln(st,
                [XN[:, i * NHC + off: i * NHC + off + w] for i in range(8)],
                [XN[:, i * NHC + off: i * NHC + off + w] for i in range(8)],
                w, 1024, ONEF[:, :], [SQ[:, 0:w], SQ[:, 512:512 + w]],
                PSA[:, 0:w], PSB[:, 0:w], MST[:, 0:w], RST[:, 0:w], 1e-5,
                [f"XN{i}" for i in range(8)], [f"XN{i}" for i in range(8)],
                ["SQ0", "SQ1"], ["PSA", "PSB"], "MST", "RST")
    for i in range(8):
        em.dma("sync", o_xn[i], XN[:, i * NHC:(i + 1) * NHC], reads=(f"XN{i}",))

    THRN = st.sb("THRN", [128, 2])

    def quant2(tiles, dst_off_fn, widx, thr_col):
        e_ts(st, THRN[:, thr_col:thr_col + 1], THR[:, thr_col:thr_col + 1],
             -1.0, None, ALU.mult, reads=(f"thr{widx}",), writes=(f"thrn{widx}",))
        for j, (dap, tw) in enumerate(tiles):
            em.dma("gpsimd", QW[:, 0:tw], dap, writes=("QW",))
            e_ts(st, QSB[:, 0:tw], QW[:, 0:tw], THR[:, thr_col:thr_col + 1],
                 None, ALU.is_gt, reads=("QW", f"thr{widx}"), writes=("QSBp",))
            e_ts(st, QSB[:, 2048:2048 + tw], QW[:, 0:tw],
                 THRN[:, thr_col:thr_col + 1], None, ALU.is_lt,
                 reads=("QW", f"thrn{widx}"), writes=("QSBn",))
            e_tt(st, WQ[:, dst_off_fn(j): dst_off_fn(j) + tw],
                 QSB[:, 0:tw], QSB[:, 2048:2048 + tw], ALU.subtract,
                 reads=("QSBp", "QSBn"), writes=("WQ",))

    quant2(w1_tiles, lambda j: (j // 2) * 4096 + (j % 2) * 2048, 1, 0)

    bchunks = real_chunks(256)
    hre = H[:, 0:32 * 256].rearrange("q (h c) -> q h c", c=256)
    for ci, (hoff, w, roff) in enumerate(bchunks):
        for dc in range(8):
            e_copy(st, RH[:, dc * 256: dc * 256 + w],
                   XN[:, dc * NHC + hoff: dc * NHC + hoff + w],
                   reads=(f"XN{dc}",), writes=("RH",))
        for hc in range(32):
            ps = (PSC, PSD)[hc % 2]
            psk = ("PSC", "PSD")[hc % 2]

            def mmfn(e, hc=hc, w=w, ps=ps):
                inst = None
                for dc in range(8):
                    inst = e.matmul(ps[:, 0:w],
                                    WQ[:, dc * 4096 + hc * 128: dc * 4096 + (hc + 1) * 128],
                                    RH[:, dc * 256: dc * 256 + w],
                                    start=(dc == 0), stop=(dc == 7))
                return inst

            em.add("tensor", mmfn, reads=("WQ", "RH"), writes=(psk,))
            e_act(st, H[:, hc * 256: hc * 256 + w], ps[:, 0:w], AF.Gelu,
                  reads=(psk,), writes=(f"H{hc}",))
        emit_ln(st, [H[:, hc * 256: hc * 256 + w] for hc in range(32)],
                [H[:, hc * 256: hc * 256 + w] for hc in range(32)],
                w, 4096, ONEB[:, :], [SQ[:, 0:w], SQ[:, 512:512 + w]],
                PSA[:, 0:w], PSB[:, 0:w], MST[:, 0:w], RST[:, 0:w], 1e-5,
                [f"H{hc}" for hc in range(32)], [f"H{hc}" for hc in range(32)],
                ["SQ0", "SQ1"], ["PSA", "PSB"], "MST", "RST",
                g_aps=[BLN[:, hc:hc + 1] for hc in range(32)],
                b_aps=[BLN[:, 32 + hc:33 + hc].broadcast_to([128, w])
                       for hc in range(32)],
                key_gb="bln", ones_sq=ONEF[:, :])
        em.dma("sync", hln[:, :, roff:roff + w], hre[:, :, 0:w],
               reads=tuple(f"H{hc}" for hc in range(32)), writes=("hlnD",))

    gamma_pass(2, w2_tiles, 16, 1, 4096 * 1024)
    quant2(w2_tiles, lambda j: j * 1024, 2, 1)
    for ci, (hoff, w, roff) in enumerate(bchunks):
        em.dma("sync", hre[:, :, 0:w], hln[:, :, roff:roff + w],
               reads=("hlnD",), writes=tuple(f"H{hc}" for hc in range(32)))
        for dout in range(8):
            ps = (PSC, PSD)[dout % 2]
            psk = ("PSC", "PSD")[dout % 2]

            def mmfn(e, dout=dout, w=w, ps=ps):
                inst = None
                for hc in range(32):
                    inst = e.matmul(ps[:, 0:w],
                                    WQ[:, hc * 1024 + dout * 128: hc * 1024 + (dout + 1) * 128],
                                    H[:, hc * 256: hc * 256 + w],
                                    start=(hc == 0), stop=(hc == 31))
                return inst

            em.add("tensor", mmfn,
                   reads=("WQ",) + tuple(f"H{hc}" for hc in range(32)),
                   writes=(psk,))
            bo = BO[:, (dout % 2) * 512:(dout % 2) * 512 + w]
            e_act(st, bo, ps[:, 0:w], AF.Copy, reads=(psk,),
                  writes=(f"BO{dout % 2}",))
            em.dma("sync", o_bit[dout][:, roff:roff + w], bo,
                   reads=(f"BO{dout % 2}",))
    return st


def build_l0b():
    st = Stage("l0b")
    em = st.em
    xnh = st.inp("xnh", (8, 128, NHC))
    ipw = st.inp("in_proj_w", (1024, 4096))
    xpw = st.inp("x_proj_w", (2048, 96))
    dtw = st.inp("dt_proj_w", (64, 2048))
    lg = st.inp("lg", (128, 16))
    cw = st.inp("cw", (128, 64))
    cb = st.inp("cb", (128, 16))
    dtb = st.inp("dtb", (128, 16))
    ones_f = st.inp("ones_f", (128, 128))
    o_xc = st.out("o_xc", (16, 128, NRC), BF16)
    o_z = st.out("o_z", (16, 128, NRC), BF16)
    o_dt = st.out("o_dt", (16, 128, NRC))
    o_bc = st.out("o_bc", (32, NRC))

    XN = st.sb("XN", [128, 8 * NHC])
    LB = st.sb("LB", [128, 8 * NHC], BF16)
    IPW = st.sb("IPW", [128, 2 * 8 * 512], BF16)
    XIZ = st.sb("XIZ", [128, 8 * NHC], BF16)
    XC = st.sb("XC", [128, 2 * NRC], BF16)
    XPW = st.sb("XPW", [128, 16 * 96], BF16)
    DTW = st.sb("DTW", [64, 2048], BF16)
    XD = st.sb("XD", [128, NRC])
    DTRB = st.sb("DTRB", [64, NRC], BF16)
    DTV = st.sb("DTV", [128, NRC])
    DSC = st.sb("DSC", [128, 2 * NRC])
    ONEF = st.sb("ONEF", [128, 128])
    MST = st.sb("MST", [128, 512])
    RST = st.sb("RST", [128, 512])
    SQ = st.sb("SQ", [128, 1024])
    LG = st.sb("LG", [128, 16])
    CW = st.sb("CW", [128, 64])
    CB = st.sb("CB", [128, 16])
    DTB = st.sb("DTB", [128, 16])
    PSA = st.ps("PSA", [128, 512])
    PSB = st.ps("PSB", [128, 512])
    PSC = st.ps("PSC", [128, 512])
    PSD = st.ps("PSD", [128, 512])
    PSX = [st.ps(f"PSX{i}", [128, 512]) for i in range(4)]

    em.dma("sync", ONEF[:, :], ones_f, writes=("onef",))
    for t, ap, k in ((LG, lg, "lg"), (CW, cw, "cw"), (CB, cb, "cb"),
                     (DTB, dtb, "dtb")):
        em.dma("sync", t[:, :], ap, writes=(k,))
    for i in range(8):
        em.dma("sync", XN[:, i * NHC:(i + 1) * NHC], xnh[i], writes=(f"XN{i}",))
    for k in range(16):
        em.dma("gpsimd", XPW[:, k * 96:(k + 1) * 96],
               xpw[k * 128:(k + 1) * 128, :], writes=("XPW",))
    for k in range(16):
        em.dma("gpsimd", DTW[:, k * 128:(k + 1) * 128],
               dtw[:, k * 128:(k + 1) * 128], writes=("DTW",))

    for off, w in chunks_of(NHC, 512):
        emit_ln(st,
                [XN[:, i * NHC + off: i * NHC + off + w] for i in range(8)],
                [LB[:, i * NHC + off: i * NHC + off + w] for i in range(8)],
                w, 1024, ONEF[:, :], [SQ[:, 0:w], SQ[:, 512:512 + w]],
                PSA[:, 0:w], PSB[:, 0:w], MST[:, 0:w], RST[:, 0:w], 1e-5,
                [f"XN{i}" for i in range(8)], [f"LB{i}" for i in range(8)],
                ["SQ0", "SQ1"], ["PSA", "PSB"], "MST", "RST",
                g_aps=[LG[:, i:i + 1] for i in range(8)],
                b_aps=[LG[:, 8 + i:9 + i].broadcast_to([128, w])
                       for i in range(8)],
                key_gb="lg")

    mvch = chunks_of(NHC, 512)
    rch = [(a * 1027 + 3, 1024, a * 1024) for a in range(2)]
    for g in range(8):
        gslot = g % 2
        for k in range(8):
            em.dma("gpsimd",
                   IPW[:, gslot * 4096 + k * 512:(gslot * 4096) + (k + 1) * 512],
                   ipw[k * 128:(k + 1) * 128, g * 512:(g + 1) * 512],
                   writes=(f"IPW{gslot}",))
        for cc in range(4):
            gc = g * 4 + cc
            slot = gc % 8
            tkey = f"XIZ{slot}"
            for mi, (off, w) in enumerate(mvch):
                ps = (PSC, PSD)[mi % 2]
                psk = ("PSC", "PSD")[mi % 2]

                def mmfn(e, g=g, cc=cc, off=off, w=w, ps=ps, gslot=gslot):
                    inst = None
                    for k in range(8):
                        inst = e.matmul(
                            ps[:, 0:w],
                            IPW[:, gslot * 4096 + k * 512 + cc * 128:
                                gslot * 4096 + k * 512 + (cc + 1) * 128],
                            LB[:, k * NHC + off: k * NHC + off + w],
                            start=(k == 0), stop=(k == 7))
                    return inst

                em.add("tensor", mmfn,
                       reads=(f"IPW{gslot}",) + tuple(f"LB{k}" for k in range(8)),
                       writes=(psk,))
                e_act(st, XIZ[:, slot * NHC + off: slot * NHC + off + w],
                      ps[:, 0:w], AF.Copy, reads=(psk,), writes=(tkey,))
            if gc < 16:
                xcs = gc % 2
                xck = f"XC{xcs}"
                for a, (hoff, w1k, roff) in enumerate(rch):
                    tile = XIZ[:, slot * NHC:(slot + 1) * NHC]
                    dst = XC[:, xcs * NRC + roff: xcs * NRC + roff + 1024]
                    e_ts(st, dst, tile[:, hoff - 3: hoff - 3 + 1024],
                         CW[:, gc * 4: gc * 4 + 1], None, ALU.mult,
                         reads=(tkey, "cw"), writes=(xck,))
                    for j in (1, 2, 3):
                        e_stt(st, dst, tile[:, hoff - 3 + j: hoff - 3 + j + 1024],
                              CW[:, gc * 4 + j: gc * 4 + j + 1], dst,
                              ALU.mult, ALU.add, reads=(tkey, "cw", xck),
                              writes=(xck,))
                e_act(st, XC[:, xcs * NRC: (xcs + 1) * NRC],
                      XC[:, xcs * NRC: (xcs + 1) * NRC], AF.Silu,
                      bias=CB[:, gc:gc + 1], reads=(xck, "cb"), writes=(xck,))
                em.dma("sync", o_xc[gc], XC[:, xcs * NRC: (xcs + 1) * NRC],
                       reads=(xck,))
                for mi, (off, w) in enumerate(chunks_of(NRC, 512)):
                    em.add("tensor",
                           lambda e, gc=gc, mi=mi, off=off, w=w: e.matmul(
                               PSX[mi][0:96, 0:w],
                               XPW[:, gc * 96:(gc + 1) * 96],
                               XC[:, (gc % 2) * NRC + off: (gc % 2) * NRC + off + w],
                               start=(gc == 0), stop=(gc == 15)),
                           reads=("XPW", xck), writes=(f"PSX{mi}",))
            else:
                zc = gc - 16
                em.dma("sync", o_z[zc],
                       XIZ[:, slot * NHC:(slot + 1) * NHC]
                       .rearrange("q (a p) -> q a p", p=1027)[:, :, 3:1027],
                       reads=(tkey,))

    for mi, (off, w) in enumerate(chunks_of(NRC, 512)):
        e_act(st, XD[0:96, off:off + w], PSX[mi][0:96, 0:w], AF.Copy,
              reads=(f"PSX{mi}",), writes=("XD",))
    em.dma("sync", o_bc, XD[64:96, :], reads=("XD",))
    e_copy(st, DTRB[:, :], XD[0:64, :], reads=("XD",), writes=("DTRB",))
    for dc in range(16):
        for mi, (off, w) in enumerate(chunks_of(NRC, 512)):
            ps = (PSC, PSD)[mi % 2]
            psk = ("PSC", "PSD")[mi % 2]
            em.add("tensor",
                   lambda e, dc=dc, off=off, w=w, ps=ps: e.matmul(
                       ps[:, 0:w], DTW[:, dc * 128:(dc + 1) * 128],
                       DTRB[:, off:off + w], start=True, stop=True),
                   reads=("DTW", "DTRB"), writes=(psk,))
            e_ts(st, DTV[:, off:off + w], ps[:, 0:w], DTB[:, dc:dc + 1],
                 None, ALU.add, reads=(psk, "dtb"), writes=("DTV",))
        e_act(st, DSC[:, 0:NRC], DTV[:, :], AF.Abs, reads=("DTV",),
              writes=("DSCa",))
        e_act(st, DSC[:, 0:NRC], DSC[:, 0:NRC], AF.Exp,
              scale=st.const(-1.0), reads=("DSCa", "consts"), writes=("DSCa",))
        e_act(st, DSC[:, 0:NRC], DSC[:, 0:NRC], AF.Ln,
              bias=st.const(1.0), reads=("DSCa", "consts"), writes=("DSCa",))
        e_ts(st, DSC[:, NRC:2 * NRC], DTV[:, :], 0.0, None, ALU.max,
             reads=("DTV",), writes=("DSCr",))
        e_tt(st, DTV[:, :], DSC[:, 0:NRC], DSC[:, NRC:2 * NRC], ALU.add,
             reads=("DSCa", "DSCr"), writes=("DTV",))
        em.dma("sync", o_dt[dc], DTV[:, :], reads=("DTV",))
    return st


TCH = 2048


def build_l0c():
    st = Stage("l0c")
    em = st.em
    xcs = st.inp("xcs", (4, 128, TT), BF16)
    dts = st.inp("dts", (4, 128, TT))
    zs = st.inp("zs", (4, 128, TT), BF16)
    bcs = st.inp("bcs", (32, TT))
    aap = st.inp("aap", (128, 64))
    dap = st.inp("dap", (128, 4))
    h0 = st.inp("h0", (128, 64))
    o_yg = st.out("o_yg", (4, 128, TT), BF16)
    o_hT = st.out("o_hT", (128, 64))

    DTS = [st.sb(f"DTS{i}", [128, TCH]) for i in range(4)]
    XCS = [st.sb(f"XCS{i}", [128, TCH], BF16) for i in range(4)]
    DTX = [st.sb(f"DTX{i}", [128, TCH], BF16) for i in range(4)]
    Y = [st.sb(f"Y{i}", [128, TCH]) for i in range(4)]
    DA = [st.sb(f"DA{i}", [128, TCH]) for i in range(2)]
    DB = [st.sb(f"DB{i}", [128, TCH], BF16) for i in range(2)]
    H = [st.sb(f"HH{i}", [128, TCH]) for i in range(2)]
    TMP = [st.sb(f"TMP{i}", [128, TCH], BF16) for i in range(2)]
    BT = [st.sb(f"BT{i}", [128, TCH]) for i in range(2)]
    CT = [st.sb(f"CT{i}", [128, TCH]) for i in range(2)]
    Z = [st.sb(f"Z{i}", [128, TCH], BF16) for i in range(2)]
    YG = [st.sb(f"YG{i}", [128, TCH], BF16) for i in range(2)]
    AAP = st.sb("AAP", [128, 64])
    DAP = st.sb("DAP", [128, 4])
    HC = st.sb("HC", [128, 64])

    em.dma("sync", AAP[:, :], aap, writes=("aap",))
    e_act(st, AAP[:, :], AAP[:, :], AF.Exp, reads=("aap",), writes=("aap",))
    e_ts(st, AAP[:, :], AAP[:, :], -1.0, None, ALU.mult, reads=("aap",),
         writes=("aap",))
    em.dma("sync", DAP[:, :], dap, writes=("dap",))
    em.dma("sync", HC[:, :], h0, writes=tuple(f"HC{j}" for j in range(64)))

    nt = TT // TCH
    for tc in range(nt):
        cs = slice(tc * TCH, (tc + 1) * TCH)
        for dc in range(4):
            em.dma("sync", DTS[dc][:, :], dts[dc][:, cs], writes=(f"DTS{dc}",))
            em.dma("sync", XCS[dc][:, :], xcs[dc][:, cs], writes=(f"XCS{dc}",))
            e_tt(st, DTX[dc][:, :], DTS[dc][:, :], XCS[dc][:, :], ALU.mult,
                 reads=(f"DTS{dc}", f"XCS{dc}"), writes=(f"DTX{dc}",),
                 eng="gpsimd")
        for n in range(16):
            nb = n % 2
            em.dma("sync", BT[nb][:, :],
                   bcs[n:n + 1, cs].broadcast_to([128, TCH]), writes=(f"BT{nb}",))
            em.dma("sync", CT[nb][:, :],
                   bcs[16 + n:17 + n, cs].broadcast_to([128, TCH]),
                   writes=(f"CT{nb}",))
            for dc in range(4):
                db = dc % 2
                ch = n * 4 + dc
                e_act(st, DA[db][:, :], DTS[dc][:, :], AF.Exp,
                      scale=AAP[:, dc * 16 + n: dc * 16 + n + 1],
                      reads=(f"DTS{dc}", "aap"), writes=(f"DA{db}",))
                e_tt(st, DB[db][:, :], DTX[dc][:, :], BT[nb][:, :], ALU.mult,
                     reads=(f"DTX{dc}", f"BT{nb}"), writes=(f"DB{db}",),
                     eng="gpsimd")
                em.add("vector",
                       lambda e, db=db, ch=ch: e.tensor_tensor_scan(
                           H[db][:, :], DA[db][:, :], DB[db][:, :],
                           HC[:, ch:ch + 1], ALU.mult, ALU.add),
                       reads=(f"DA{db}", f"DB{db}", f"HC{ch}"),
                       writes=(f"HH{db}",))
                e_copy(st, HC[:, ch:ch + 1], H[db][:, TCH - 1:TCH],
                       reads=(f"HH{db}",), writes=(f"HC{ch}",))
                e_tt(st, TMP[db][:, :], H[db][:, :], CT[nb][:, :], ALU.mult,
                     reads=(f"HH{db}", f"CT{nb}"), writes=(f"TMP{db}",),
                     eng="gpsimd")
                if n == 0:
                    e_copy(st, Y[dc][:, :], TMP[db][:, :],
                           reads=(f"TMP{db}",), writes=(f"Y{dc}",))
                else:
                    e_tt(st, Y[dc][:, :], Y[dc][:, :], TMP[db][:, :], ALU.add,
                         reads=(f"Y{dc}", f"TMP{db}"), writes=(f"Y{dc}",))
        for dc in range(4):
            db = dc % 2
            em.dma("sync", Z[db][:, :], zs[dc][:, cs], writes=(f"Z{db}",))
            e_act(st, Z[db][:, :], Z[db][:, :], AF.Silu,
                  reads=(f"Z{db}",), writes=(f"Z{db}",))
            e_stt(st, Y[dc][:, :], XCS[dc][:, :], DAP[:, dc:dc + 1],
                  Y[dc][:, :], ALU.mult, ALU.add,
                  reads=(f"XCS{dc}", "dap", f"Y{dc}"), writes=(f"Y{dc}",))
            e_tt(st, YG[db][:, :], Y[dc][:, :], Z[db][:, :], ALU.mult,
                 reads=(f"Y{dc}", f"Z{db}"), writes=(f"YG{db}",))
            em.dma("sync", o_yg[dc][:, cs], YG[db][:, :], reads=(f"YG{db}",))
    em.dma("sync", o_hT, HC[:, :], reads=tuple(f"HC{j}" for j in range(64)))
    return st


def build_l0d():
    st = Stage("l0d")
    em = st.em
    ygs = st.inp("ygs", (16, 128, NRC), BF16)
    xn8 = st.inp("xn8", (8, 128, NRC))
    bit8 = st.inp("bit8", (8, 128, NRC))
    xs8 = st.inp("xs8", (8, 128, NRC))
    opw = st.inp("out_proj_w", (2048, 1024))
    fw1 = st.inp("ffn_w1", (1024, 4096))
    fw2 = st.inp("ffn_w2", (4096, 1024))
    lg2 = st.inp("lg2", (128, 16))
    fb1 = st.inp("fb1", (128, 32))
    fb2 = st.inp("fb2", (128, 8))
    ones_f = st.inp("ones_f", (128, 128))
    o_x1 = st.out("o_x1", (8, 128, NRC))
    o_x1pre = st.out("o_x1pre", (8, 128, NRC))

    X1 = st.sb("X1", [128, 8 * NRC])
    H2B = st.sb("H2B", [128, 8 * NRC], BF16)
    WBIG = st.sb("WBIG", [128, 16384], BF16)
    YGK = st.sb("YGK", [128, 8 * 512], BF16)
    W1G = st.sb("W1G", [128, 2 * 8 * 512], BF16)
    H1 = st.sb("H1", [128, 32 * 512], BF16)
    XSB = st.sb("XSB", [128, 2 * 512])
    BIT = st.sb("BIT", [128, 2 * 512])
    OUTS = st.sb("OUTS", [128, 2 * 512])
    ONEF = st.sb("ONEF", [128, 128])
    MST = st.sb("MST", [128, 512])
    RST = st.sb("RST", [128, 512])
    SQ = st.sb("SQ", [128, 1024])
    LG2 = st.sb("LG2", [128, 16])
    FB1 = st.sb("FB1", [128, 32])
    FB2 = st.sb("FB2", [128, 8])
    PSC = st.ps("PSC", [128, 512])
    PSD = st.ps("PSD", [128, 512])
    PSA = st.ps("PSA", [128, 512])
    PSB = st.ps("PSB", [128, 512])
    PSE = [st.ps(f"PSE{i}", [128, 512]) for i in range(4)]

    em.dma("sync", ONEF[:, :], ones_f, writes=("onef",))
    for t, ap, k in ((LG2, lg2, "lg2"), (FB1, fb1, "fb1"), (FB2, fb2, "fb2")):
        em.dma("sync", t[:, :], ap, writes=(k,))
    for i in range(8):
        em.dma("sync", X1[:, i * NRC:(i + 1) * NRC], xn8[i], writes=(f"X1_{i}",))
    for k in range(16):
        em.dma("gpsimd", WBIG[:, k * 1024:(k + 1) * 1024],
               opw[k * 128:(k + 1) * 128, :], writes=("WBIG",))

    for mi, (off, w) in enumerate(chunks_of(NRC, 512)):
        for half in range(2):
            for kk in range(8):
                k = half * 8 + kk
                em.dma("sync", YGK[:, kk * 512:kk * 512 + w],
                       ygs[k][:, off:off + w], writes=(f"YGK{kk}",))
            for cc in range(8):
                ps = (PSC, PSD, PSA, PSB, PSE[0], PSE[1], PSE[2], PSE[3])[cc]
                psk = ("PSC", "PSD", "PSA", "PSB", "PSE0", "PSE1", "PSE2", "PSE3")[cc]

                def mmfn(e, cc=cc, w=w, ps=ps, half=half):
                    inst = None
                    for kk in range(8):
                        k = half * 8 + kk
                        inst = e.matmul(ps[:, 0:w],
                                        WBIG[:, k * 1024 + cc * 128: k * 1024 + (cc + 1) * 128],
                                        YGK[:, kk * 512:kk * 512 + w],
                                        start=(half == 0 and kk == 0),
                                        stop=(half == 1 and kk == 7))
                    return inst

                em.add("tensor", mmfn,
                       reads=("WBIG",) + tuple(f"YGK{kk}" for kk in range(8)),
                       writes=(psk,))
                if half == 1:
                    e_tt(st, X1[:, cc * NRC + off: cc * NRC + off + w],
                         X1[:, cc * NRC + off: cc * NRC + off + w], ps[:, 0:w],
                         ALU.add, reads=(f"X1_{cc}", psk), writes=(f"X1_{cc}",))

    for i in range(8):
        em.dma("sync", o_x1pre[i], X1[:, i * NRC:(i + 1) * NRC],
               reads=(f"X1_{i}",))

    for off, w in chunks_of(NRC, 512):
        emit_ln(st,
                [X1[:, i * NRC + off: i * NRC + off + w] for i in range(8)],
                [H2B[:, i * NRC + off: i * NRC + off + w] for i in range(8)],
                w, 1024, ONEF[:, :], [SQ[:, 0:w], SQ[:, 512:512 + w]],
                PSA[:, 0:w], PSB[:, 0:w], MST[:, 0:w], RST[:, 0:w], 1e-5,
                [f"X1_{i}" for i in range(8)], [f"H2B{i}" for i in range(8)],
                ["SQ0", "SQ1"], ["PSA", "PSB"], "MST", "RST",
                g_aps=[LG2[:, i:i + 1] for i in range(8)],
                b_aps=[LG2[:, 8 + i:9 + i].broadcast_to([128, w])
                       for i in range(8)],
                key_gb="lg2")

    for mi, (off, w) in enumerate(chunks_of(NRC, 512)):
        for g in range(8):
            gs = g % 2
            for k in range(8):
                em.dma("gpsimd",
                       W1G[:, gs * 4096 + k * 512:gs * 4096 + (k + 1) * 512],
                       fw1[k * 128:(k + 1) * 128, g * 512:(g + 1) * 512],
                       writes=(f"W1G{gs}",))
            for cc in range(4):
                hc = g * 4 + cc
                ps = (PSC, PSD)[hc % 2]
                psk = ("PSC", "PSD")[hc % 2]

                def mmfn(e, cc=cc, w=w, ps=ps, gs=gs, off=off):
                    inst = None
                    for k in range(8):
                        inst = e.matmul(
                            ps[:, 0:w],
                            W1G[:, gs * 4096 + k * 512 + cc * 128:
                                gs * 4096 + k * 512 + (cc + 1) * 128],
                            H2B[:, k * NRC + off: k * NRC + off + w],
                            start=(k == 0), stop=(k == 7))
                    return inst

                em.add("tensor", mmfn,
                       reads=(f"W1G{gs}",) + tuple(f"H2B{k}" for k in range(8)),
                       writes=(psk,))
                e_act(st, H1[:, hc * 512: hc * 512 + w], ps[:, 0:w],
                      AF.Gelu_apprx_tanh, bias=FB1[:, hc:hc + 1],
                      reads=(psk, "fb1"), writes=(f"H1_{hc}",))
        for half in range(2):
            for k in range(32):
                em.dma("gpsimd", WBIG[:, k * 512:(k + 1) * 512],
                       fw2[k * 128:(k + 1) * 128, half * 512:(half + 1) * 512],
                       writes=("WBIG",))
            for dq in range(4):
                dcc = half * 4 + dq
                ps = (PSC, PSD)[dq % 2]
                psk = ("PSC", "PSD")[dq % 2]
                em.dma("sync", XSB[:, (dq % 2) * 512:(dq % 2) * 512 + w],
                       xs8[dcc][:, off:off + w], writes=(f"XSB{dq % 2}",))
                em.dma("sync", BIT[:, (dq % 2) * 512:(dq % 2) * 512 + w],
                       bit8[dcc][:, off:off + w], writes=(f"BIT{dq % 2}",))

                def mmfn(e, dq=dq, w=w, ps=ps):
                    inst = None
                    for k in range(32):
                        inst = e.matmul(ps[:, 0:w],
                                        WBIG[:, k * 512 + dq * 128: k * 512 + (dq + 1) * 128],
                                        H1[:, k * 512:k * 512 + w],
                                        start=(k == 0), stop=(k == 31))
                    return inst

                em.add("tensor", mmfn,
                       reads=("WBIG",) + tuple(f"H1_{k}" for k in range(32)),
                       writes=(psk,))
                ou = OUTS[:, (dq % 2) * 512:(dq % 2) * 512 + w]
                e_stt(st, ou, ps[:, 0:w], FB2[:, dcc:dcc + 1],
                      X1[:, dcc * NRC + off: dcc * NRC + off + w], ALU.add,
                      ALU.add, reads=(psk, "fb2", f"X1_{dcc}"),
                      writes=(f"OUTS{dq % 2}",))
                e_tt(st, ou, ou, XSB[:, (dq % 2) * 512:(dq % 2) * 512 + w],
                     ALU.add, reads=(f"OUTS{dq % 2}", f"XSB{dq % 2}"),
                     writes=(f"OUTS{dq % 2}",))
                e_tt(st, ou, ou, BIT[:, (dq % 2) * 512:(dq % 2) * 512 + w],
                     ALU.add, reads=(f"OUTS{dq % 2}", f"BIT{dq % 2}"),
                     writes=(f"OUTS{dq % 2}",))
                em.dma("sync", o_x1[dcc][:, off:off + w], ou,
                       reads=(f"OUTS{dq % 2}",))
    return st


import bass_rust as _br
IOA = _br.IndirectOffsetOnAxis
MOE_CAP = 512


def build_l1():
    st = Stage("l1")
    em = st.em
    x1s = st.inp("x1s", (8, 128, NRC))
    ipwT = st.inp("ipwT", (1024, 3072))
    opwT = st.inp("opwT", (1024, 1024))
    gw = st.inp("gw", (1024, 8))
    gbcol = st.inp("gbcol", (8, 1))
    w1 = st.inp("w1", (8, 1024, 4096))
    b1h = st.inp("b1h", (8, 128, 32))
    w2 = st.inp("w2", (8, 4096, 1024))
    b2h = st.inp("b2h", (8, 128, 8))
    ones_f = st.inp("ones_f", (128, 128))
    identf = st.inp("identf", (128, 128))
    identb = st.inp("identb", (128, 128), BF16)
    lstrict = st.inp("lstrict", (128, 128))
    posi = st.inp("posi", (128, 16), mybir.dt.int32)
    e512 = st.inp("e512", (128, 8))
    oobi = st.inp("oobi", (8, MOE_CAP), mybir.dt.int32)
    o_x2 = st.out("o_x2", (8, 128, NRC))

    xn_tok = st.nc.dram_tensor("xn_tok", [NRC, 1024], BF16).ap()
    moe_out = st.nc.dram_tensor("moe_out", [NRC, 1024], F32).ap()
    idxl = st.nc.dram_tensor("idxl", [8 * MOE_CAP, 1], mybir.dt.int32).ap()

    X1 = st.sb("X1", [128, 8 * NRC])
    AR1 = st.sb("AR1", [128, 8 * NRC], BF16)
    AR2 = st.sb("AR2", [128, 8 * NRC], BF16)
    WA = st.sb("WA", [128, 8192], BF16)
    QKV = st.sb("QKV", [128, 2 * 3072], BF16)
    SC = st.sb("SCR", [128, 3072])
    LGT = st.sb("LGT", [128, 16 * 8])
    EQ = st.sb("EQ", [128, 16 * 8])
    RNK = st.sb("RNK", [128, 16 * 8])
    CNT = st.sb("CNT", [128, 16 * 8])
    EXC = st.sb("EXC", [128, 16 * 8])
    Z8 = st.sb("Z8", [128, 8])
    DESTF = st.sb("DESTF", [128, 1])
    DESTI = st.sb("DESTI", [128, 1], mybir.dt.int32)
    IX = st.sb("IX", [128, 32], mybir.dt.int32)
    ONEF = st.sb("ONEF", [128, 128])
    IDF = st.sb("IDF", [128, 128])
    IDB = st.sb("IDB", [128, 128], BF16)
    LSTR = st.sb("LSTR", [128, 128])
    POSI = st.sb("POSI", [128, 16], mybir.dt.int32)
    E512 = st.sb("E512", [128, 8])
    GBC = st.sb("GBC", [8, 1])
    B1E = st.sb("B1E", [128, 32])
    B2E = st.sb("B2E", [128, 8])
    MST = st.sb("MST", [128, 512])
    RST = st.sb("RST", [128, 512])
    SQ = st.sb("SQ", [128, 1024])
    ZT = st.sb("ZT", [128, 1024])
    PS = [st.ps(f"PS{i}", [128, 512]) for i in range(8)]

    for t, ap, k in ((ONEF, ones_f, "onef"), (IDF, identf, "idf"),
                     (IDB, identb, "idb"), (LSTR, lstrict, "lstr"),
                     (POSI, posi, "posi"), (E512, e512, "e512"),
                     (GBC, gbcol, "gbc")):
        em.dma("sync", t[:, :], ap, writes=(k,))
    for i in range(8):
        em.dma("sync", X1[:, i * NRC:(i + 1) * NRC], x1s[i], writes=(f"X1_{i}",))
    st.em.add("gpsimd", lambda e: e.memset(ZT[:, :], 0.0), writes=("ZT",))
    for pc in range(16):
        em.dma("sync", moe_out[pc * 128:(pc + 1) * 128, :], ZT[:, :],
               reads=("ZT",), writes=("moed",))
    em.dma("sync", idxl[:, 0:1], oobi.rearrange("e c -> (e c)").unsqueeze(1),
           writes=("idxl",))

    for off, w in chunks_of(NRC, 512):
        emit_ln(st,
                [X1[:, i * NRC + off: i * NRC + off + w] for i in range(8)],
                [AR1[:, i * NRC + off: i * NRC + off + w] for i in range(8)],
                w, 1024, ONEF[:, :], [SQ[:, 0:w], SQ[:, 512:512 + w]],
                PS[0][:, 0:w], PS[1][:, 0:w], MST[:, 0:w], RST[:, 0:w], 1e-5,
                [f"X1_{i}" for i in range(8)], [f"XN{i}" for i in range(8)],
                ["SQ0", "SQ1"], ["PS0", "PS1"], "MST", "RST")

    for pc in range(16):
        for dc in range(8):
            em.add("tensor",
                   lambda e, pc=pc, dc=dc: e.transpose(
                       PS[2][:, 0:128].bitcast(BF16)[:, 0:128],
                       AR1[:, dc * NRC + pc * 128: dc * NRC + (pc + 1) * 128],
                       IDB[:, :]),
                   reads=(f"XN{dc}", "idb"), writes=("PS2",))
            e_copy(st, QKV[:, dc * 128:(dc + 1) * 128],
                   PS[2][:, 0:128].bitcast(BF16)[:, 0:128],
                   reads=("PS2",), writes=("TOKS",))
        em.dma("sync", xn_tok[pc * 128:(pc + 1) * 128, :],
               QKV[:, 0:1024], reads=("TOKS",), writes=("xn_tok",))

    ipw_loaded = [False]
    IPW = AR2
    CTF_OFF = 0

    for p8 in range(8):
        for a in range(2):
            for fc in range(6):
                ps = PS[2 + (fc % 2)]
                psk = f"PS{2 + (fc % 2)}"
                for k in range(8):
                    em.dma("gpsimd", WA[:, k * 512:(k + 1) * 512],
                           ipwT[k * 128:(k + 1) * 128, fc * 512:(fc + 1) * 512],
                           writes=(f"WAk{k}",))

                def mmfn(e, a=a, p8=p8, ps=ps):
                    inst = None
                    for k in range(8):
                        inst = e.matmul(
                            ps[:, 0:512],
                            AR1[:, k * NRC + a * 1024 + p8 * 128:
                                k * NRC + a * 1024 + (p8 + 1) * 128],
                            WA[:, k * 512:(k + 1) * 512],
                            start=(k == 0), stop=(k == 7))
                    return inst

                em.add("tensor", mmfn,
                       reads=tuple(f"WAk{k}" for k in range(8))
                       + tuple(f"XN{k}" for k in range(8)), writes=(psk,))
                e_copy(st, QKV[:, a * 3072 + fc * 512:a * 3072 + (fc + 1) * 512],
                       ps[:, 0:512], reads=(psk,), writes=(f"QKV{a}",),
                       eng="scalar")
        qs = lambda a: QKV[:, a * 3072:(a + 1) * 3072].rearrange("q (h c d) -> q h c d", c=3, d=64)
        for a in range(2):
            for b in range(2):
                col = (a * 2 + b) * 16
                e_tt(st, SC[:, 0:1024].rearrange("q (h d) -> q h d", d=64),
                     qs(a)[:, :, 0, :], qs(b)[:, :, 1, :], ALU.mult,
                     reads=(f"QKV{a}", f"QKV{b}"), writes=("SCp",))
                st.em.add("vector",
                          lambda e, col=col: e.tensor_reduce(
                              SC[:, 2048 + col:2048 + col + 16],
                              SC[:, 0:1024].rearrange("q (h d) -> q h d", d=64),
                              AX.X, ALU.add),
                          reads=("SCp",), writes=("SCs",))
        for a in range(2):
            s0 = SC[:, 2048 + (a * 2) * 16: 2048 + (a * 2) * 16 + 16]
            s1 = SC[:, 2048 + (a * 2 + 1) * 16: 2048 + (a * 2 + 1) * 16 + 16]
            mx = SC[:, 1024:1040]
            e_tt(st, mx, s0, s1, ALU.max, reads=("SCs",), writes=("SCm",))
            for j, sj in enumerate((s0, s1)):
                e_tt(st, sj, sj, mx, ALU.subtract, reads=("SCs", "SCm"),
                     writes=("SCs",))
                e_act(st, sj, sj, AF.Exp, scale=st.const(0.125),
                      reads=("SCs", "consts"), writes=("SCs",))
            e_tt(st, mx, s0, s1, ALU.add, reads=("SCs",), writes=("SCm",))
            st.em.add("vector", lambda e, mx=mx: e.reciprocal(mx, mx),
                      reads=("SCm",), writes=("SCm",))
            for sj in (s0, s1):
                e_tt(st, sj, sj, mx, ALU.mult, reads=("SCs", "SCm"),
                     writes=("SCs",))
        for a in range(2):
            p0 = SC[:, 2048 + (a * 2) * 16: 2048 + (a * 2) * 16 + 16]
            p1 = SC[:, 2048 + (a * 2 + 1) * 16: 2048 + (a * 2 + 1) * 16 + 16]
            ctx = SC[:, 0:1024]
            e_tt(st, ctx.rearrange("q (h d) -> q h d", d=64),
                 qs(0)[:, :, 2, :],
                 p0.rearrange("q (h o) -> q h o", o=1).broadcast_to([128, 16, 64]),
                 ALU.mult, reads=("QKV0", "SCs"), writes=("SCc",))
            e_tt(st, SC[:, 1024:2048].rearrange("q (h d) -> q h d", d=64),
                 qs(1)[:, :, 2, :],
                 p1.rearrange("q (h o) -> q h o", o=1).broadcast_to([128, 16, 64]),
                 ALU.mult, reads=("QKV1", "SCs"), writes=("SCc2",))
            e_tt(st, ctx, ctx, SC[:, 1024:2048], ALU.add,
                 reads=("SCc", "SCc2"), writes=("SCc",))
            for dc in range(8):
                em.add("tensor",
                       lambda e, dc=dc: e.transpose(
                           PS[4][:, 0:128], SC[:, dc * 128:(dc + 1) * 128],
                           IDF[:, :]),
                       reads=("SCc", "idf"), writes=("PS4",))
                e_copy(st,
                       AR2[:, dc * NRC + a * 1024 + p8 * 128:
                           dc * NRC + a * 1024 + (p8 + 1) * 128],
                       PS[4][:, 0:128], reads=("PS4",), writes=(f"CTF{dc}",),
                       eng="scalar")

    for k in range(8):
        em.dma("gpsimd", WA[:, k * 1024:(k + 1) * 1024],
               opwT[k * 128:(k + 1) * 128, :], writes=("WAo",))
    for mi, (off, w) in enumerate(chunks_of(NRC, 512)):
        for cc in range(8):
            ps = PS[2 + (cc % 2)]
            psk = f"PS{2 + (cc % 2)}"

            def mmfn(e, cc=cc, off=off, w=w, ps=ps):
                inst = None
                for k in range(8):
                    inst = e.matmul(ps[:, 0:w],
                                    WA[:, k * 1024 + cc * 128: k * 1024 + (cc + 1) * 128],
                                    AR2[:, k * NRC + off: k * NRC + off + w],
                                    start=(k == 0), stop=(k == 7))
                return inst

            em.add("tensor", mmfn,
                   reads=("WAo",) + tuple(f"CTF{k}" for k in range(8)),
                   writes=(psk,))
            e_tt(st, X1[:, cc * NRC + off: cc * NRC + off + w],
                 X1[:, cc * NRC + off: cc * NRC + off + w], ps[:, 0:w],
                 ALU.add, reads=(f"X1_{cc}", psk), writes=(f"X1_{cc}",))

    for k in range(8):
        em.dma("gpsimd", WA[:, 8192 - 64 + k * 8: 8192 - 64 + (k + 1) * 8],
               gw[k * 128:(k + 1) * 128, :], writes=("WAg",))
    for mi, (off, w) in enumerate(chunks_of(NRC, 512)):
        def mmg(e, off=off, w=w):
            inst = None
            for k in range(8):
                inst = e.matmul(PS[5][0:8, 0:w],
                                WA[:, 8192 - 64 + k * 8: 8192 - 64 + (k + 1) * 8],
                                AR1[:, k * NRC + off: k * NRC + off + w],
                                start=(k == 0), stop=(k == 7))
            return inst

        em.add("tensor", mmg,
               reads=("WAg",) + tuple(f"XN{k}" for k in range(8)),
               writes=("PS5",))
        e_ts(st, SC[0:8, 0:w], PS[5][0:8, 0:w], GBC[0:8, 0:1], None, ALU.add,
             reads=("PS5", "gbc"), writes=("LGF",))
        for j in range(w // 128):
            pc = (off + j * 128) // 128
            em.add("tensor",
                   lambda e, j=j: e.transpose(
                       PS[6][:, 0:8], SC[0:8, j * 128:(j + 1) * 128],
                       IDF[0:8, 0:8]),
                   reads=("LGF", "idf"), writes=("PS6",))
            e_copy(st, LGT[:, pc * 8:(pc + 1) * 8], PS[6][0:128, 0:8],
                   reads=("PS6",), writes=("LGT",), eng="scalar")
    st.em.add("gpsimd", lambda e: e.memset(Z8[:, :], 0.0), writes=("z8",))
    for pc in range(16):
        lg = LGT[:, pc * 8:(pc + 1) * 8]
        eq = EQ[:, pc * 8:(pc + 1) * 8]
        st.em.add("vector",
                  lambda e, lg=lg: e.reduce_max(DESTF[:, 0:1], lg, AX.X),
                  reads=("LGT",), writes=("DESTF",))
        e_tt(st, eq, lg, DESTF[:, 0:1].broadcast_to([128, 8]), ALU.is_ge,
             reads=("LGT", "DESTF"), writes=("EQ",))
        st.em.add("vector",
                  lambda e, eq=eq, pc=pc: e.tensor_tensor_scan(
                      RNK[:, pc * 8:(pc + 1) * 8], eq, Z8[:, :], 0.0,
                      ALU.add, ALU.add),
                  reads=("EQ", "z8"), writes=("RNKt",))
        e_ts(st, RNK[:, pc * 8:(pc + 1) * 8], RNK[:, pc * 8:(pc + 1) * 8],
             1.0, None, ALU.is_equal, reads=("RNKt",), writes=("RNKt",))
        e_tt(st, eq, eq, RNK[:, pc * 8:(pc + 1) * 8], ALU.mult,
             reads=("EQ", "RNKt"), writes=("EQ",))

    for pc in range(16):
        eq = EQ[:, pc * 8:(pc + 1) * 8]
        em.add("tensor",
               lambda e, eq=eq: e.matmul(PS[5][:, 0:8], LSTR[:, :], eq,
                                         start=True, stop=True),
               reads=("EQ", "lstr"), writes=("PS5",))
        e_copy(st, RNK[:, pc * 8:(pc + 1) * 8], PS[5][:, 0:8],
               reads=("PS5",), writes=("RNK",), eng="scalar")
        em.add("tensor",
               lambda e, eq=eq: e.matmul(PS[6][:, 0:8], ONEF[:, :], eq,
                                         start=True, stop=True),
               reads=("EQ", "onef"), writes=("PS6",))
        e_copy(st, CNT[:, pc * 8:(pc + 1) * 8], PS[6][:, 0:8],
               reads=("PS6",), writes=("CNT",), eng="scalar")
    st.em.add("gpsimd", lambda e: e.memset(SC[:, 256:272], 0.0), writes=("zr16",))
    for e_ in range(8):
        exc_v = EXC[:, :].rearrange("q (c e) -> q c e", e=8)[:, :, e_:e_ + 1].squeeze(2)
        cnt_v = CNT[:, :].rearrange("q (c e) -> q c e", e=8)[:, :, e_:e_ + 1].squeeze(2)
        st.em.add("vector",
                  lambda e, exc_v=exc_v, cnt_v=cnt_v: e.tensor_tensor_scan(
                      exc_v, cnt_v, SC[:, 256:272], 0.0, ALU.add, ALU.add),
                  reads=("CNT", "zr16"), writes=("EXC",))
    e_tt(st, EXC[:, :], EXC[:, :], CNT[:, :], ALU.subtract,
         reads=("EXC", "CNT"), writes=("EXC",))

    for pc in range(16):
        eq = EQ[:, pc * 8:(pc + 1) * 8]
        idxv = SC[:, 0:8]
        e_tt(st, idxv, RNK[:, pc * 8:(pc + 1) * 8], EXC[:, pc * 8:(pc + 1) * 8],
             ALU.add, reads=("RNK", "EXC"), writes=("idxv",))
        e_ts(st, SC[:, 8:16], idxv, float(MOE_CAP), None, ALU.is_ge,
             reads=("idxv",), writes=("ovf",))
        e_stt(st, idxv, SC[:, 8:16], 1e6, idxv, ALU.mult, ALU.add,
              reads=("ovf", "idxv"), writes=("idxv",))
        e_ts(st, SC[:, 8:16], eq, -1e6, 1e6, ALU.mult, ALU.add,
             reads=("EQ",), writes=("ovf",))
        e_tt(st, idxv, idxv, SC[:, 8:16], ALU.add, reads=("idxv", "ovf"),
             writes=("idxv",))
        e_tt(st, idxv, idxv, E512[:, :], ALU.add, reads=("idxv", "e512"),
             writes=("idxv",))
        st.em.add("vector", lambda e, idxv=idxv: e.tensor_reduce(
            DESTF[:, 0:1], idxv, AX.X, ALU.min),
            reads=("idxv",), writes=("DESTF",))
        e_copy(st, DESTI[:, :], DESTF[:, :], reads=("DESTF",), writes=("DESTI",))
        st.em.add("gpsimd",
                  lambda e, pc=pc: e.indirect_dma_start(
                      idxl, IOA(DESTI[:, :], 0), POSI[:, pc:pc + 1], None),
                  reads=("DESTI", "posi", "idxl"), writes=("idxl",), dma=True)

    for j in range(32):
        em.dma("sync", IX[:, j:j + 1], idxl[j * 128:(j + 1) * 128, 0:1],
               reads=("idxl",), writes=("IX",))
    for ex in range(8):
        em.dma("sync", B1E[:, :], b1h[ex], writes=("b1e",))
        em.dma("sync", B2E[:, :], b2h[ex], writes=("b2e",))
        for cc in range(4):
            st.em.add("gpsimd",
                      lambda e, ex=ex, cc=cc: e.indirect_dma_start(
                          AR2[:, 8192 + 0:8192 + 1024].bitcast(BF16)[:, 0:1024]
                          if False else QKV[:, 0:1024],
                          None, xn_tok, IOA(IX[:, ex * 4 + cc: ex * 4 + cc + 1], 0)),
                      reads=("IX", "xn_tok"), writes=("XGtok",), dma=True)
            for dc in range(8):
                em.add("tensor",
                       lambda e, dc=dc: e.transpose(
                           PS[2][:, 0:128].bitcast(BF16)[:, 0:128],
                           QKV[:, dc * 128:(dc + 1) * 128], IDB[:, :]),
                       reads=("XGtok", "idb"), writes=("PS2",))
                e_copy(st, AR2[:, dc * 512 + cc * 128: dc * 512 + (cc + 1) * 128],
                       PS[2][:, 0:128].bitcast(BF16)[:, 0:128],
                       reads=("PS2",), writes=(f"XGF{dc}",), eng="scalar")
        for g in range(8):
            gs = g % 2
            for k in range(8):
                em.dma("gpsimd",
                       WA[:, gs * 4096 + k * 512:gs * 4096 + (k + 1) * 512],
                       w1[ex][k * 128:(k + 1) * 128, g * 512:(g + 1) * 512],
                       writes=(f"W1G{gs}",))
            for cc in range(4):
                hc = g * 4 + cc
                ps = PS[2 + (hc % 2)]
                psk = f"PS{2 + (hc % 2)}"

                def mmfn(e, cc=cc, ps=ps, gs=gs):
                    inst = None
                    for k in range(8):
                        inst = e.matmul(
                            ps[:, 0:512],
                            WA[:, gs * 4096 + k * 512 + cc * 128:
                                gs * 4096 + k * 512 + (cc + 1) * 128],
                            AR2[:, k * 512:(k + 1) * 512],
                            start=(k == 0), stop=(k == 7))
                    return inst

                em.add("tensor", mmfn,
                       reads=(f"W1G{gs}",) + tuple(f"XGF{k}" for k in range(8)),
                       writes=(psk,))
                e_act(st, AR1[:, hc * 512:(hc + 1) * 512], ps[:, 0:512],
                      AF.Gelu, bias=B1E[:, hc:hc + 1],
                      reads=(psk, "b1e"), writes=(f"H1_{hc}",))
        for q in range(4):
            for kk in range(8):
                k = q * 8 + kk
                em.dma("gpsimd", WA[:, kk * 1024:(kk + 1) * 1024],
                       w2[ex][k * 128:(k + 1) * 128, :], writes=("W2Q",))
            for dcc in range(8):
                def mmfn(e, q=q, dcc=dcc):
                    inst = None
                    for kk in range(8):
                        k = q * 8 + kk
                        inst = e.matmul(
                            PS[dcc][:, 0:512],
                            WA[:, kk * 1024 + dcc * 128: kk * 1024 + (dcc + 1) * 128],
                            AR1[:, k * 512:(k + 1) * 512],
                            start=(q == 0 and kk == 0),
                            stop=(q == 3 and kk == 7))
                    return inst

                em.add("tensor", mmfn,
                       reads=("W2Q",) + tuple(f"H1_{q * 8 + kk}" for kk in range(8)),
                       writes=(f"PS{dcc}",))
        OUTF = AR2[:, 8192:16384].bitcast(F32)
        for dcc in range(8):
            e_ts(st, OUTF[:, dcc * 512:(dcc + 1) * 512], PS[dcc][:, 0:512],
                 B2E[:, dcc:dcc + 1], None, ALU.add,
                 reads=(f"PS{dcc}", "b2e"), writes=(f"OUTF{dcc}",))
        for cc in range(4):
            for dc in range(8):
                em.add("tensor",
                       lambda e, dc=dc, cc=cc, OUTF=OUTF: e.transpose(
                           PS[2][:, 0:128],
                           OUTF[:, dc * 512 + cc * 128: dc * 512 + (cc + 1) * 128],
                           IDF[:, :]),
                       reads=(f"OUTF{dc}", "idf"), writes=("PS2",))
                e_copy(st, SQ[:, dc * 128:(dc + 1) * 128], PS[2][:, 0:128],
                       reads=("PS2",), writes=("OTOK",), eng="scalar")
            st.em.add("gpsimd",
                      lambda e, ex=ex, cc=cc: e.indirect_dma_start(
                          moe_out, IOA(IX[:, ex * 4 + cc: ex * 4 + cc + 1], 0),
                          SQ[:, 0:1024], None),
                      reads=("IX", "OTOK", "moed"), writes=("moed",), dma=True)

    for pc in range(16):
        em.dma("sync", SC[:, 0:1024], moe_out[pc * 128:(pc + 1) * 128, :],
               reads=("moed",), writes=("MOR",))
        for dc in range(8):
            em.add("tensor",
                   lambda e, dc=dc: e.transpose(
                       PS[2][:, 0:128], SC[:, dc * 128:(dc + 1) * 128], IDF[:, :]),
                   reads=("MOR", "idf"), writes=("PS2",))
            e_tt(st, X1[:, dc * NRC + pc * 128: dc * NRC + (pc + 1) * 128],
                 X1[:, dc * NRC + pc * 128: dc * NRC + (pc + 1) * 128],
                 PS[2][:, 0:128], ALU.add, reads=(f"X1_{dc}", "PS2"),
                 writes=(f"X1_{dc}",))
    for i in range(8):
        em.dma("sync", o_x2[i], X1[:, i * NRC:(i + 1) * NRC],
               reads=(f"X1_{i}",))
    return st


L2SEG = 512
L2WARM = 512


def build_l2mem(T=8192):
    st = Stage("l2mem")
    em = st.em
    nseg = T // L2SEG
    sd = st.inp("sd", (128, T))
    o_spk = st.out("o_spk", (128, T), BF16)
    SDP = st.sb("SDP", [128, L2WARM + T])
    SPK = st.sb("SPK", [128, T], BF16)
    M = st.sb("M", [128, 2 * nseg])
    HS = st.sb("HS", [128, nseg])
    TRASH = st.sb("TRASH", [128, nseg], BF16)
    st.em.add("gpsimd", lambda e: e.memset(SDP[:, 0:L2WARM], 0.0), writes=("SDz",))
    em.dma("sync", SDP[:, L2WARM:L2WARM + T], sd, writes=("SD",))
    st.em.add("gpsimd", lambda e: e.memset(M[:, :], 0.0), writes=("M", "Mp"))
    sdv = SDP[:, :].rearrange("q (s u) -> q s u", u=L2SEG)
    spkv = SPK[:, :].rearrange("q (s u) -> q s u", u=L2SEG)
    for t in range(L2WARM + L2SEG):
        sdap = sdv[:, t // L2SEG: t // L2SEG + nseg, t % L2SEG: t % L2SEG + 1].squeeze(2)
        e_stt(st, M[:, nseg:2 * nseg], M[:, 0:nseg], 0.9, sdap,
              ALU.mult, ALU.add, reads=("M", "SD", "SDz"), writes=("Mp",))
        if t < L2WARM:
            spkap = TRASH[:, :]
            spkkey = "TRASH"
        else:
            tau = t - L2WARM
            spkap = spkv[:, 0:nseg, tau:tau + 1].squeeze(2)
            spkkey = "SPKb"
        e_ts(st, spkap, M[:, nseg:2 * nseg], 0.5, None, ALU.is_gt,
             reads=("Mp",), writes=(spkkey,))
        e_ts(st, HS[:, :], M[:, nseg:2 * nseg], 0.5, 0.5, ALU.is_gt, ALU.mult,
             reads=("Mp",), writes=("HS",))
        e_tt(st, M[:, 0:nseg], M[:, nseg:2 * nseg], HS[:, :], ALU.subtract,
             reads=("Mp", "HS"), writes=("M",))
    em.dma("sync", o_spk, SPK[:, :], reads=("SPKb",))
    return st


def build_l2b():
    import os
    PHASE = int(os.environ.get("L2B_PHASE", 5))
    st = Stage("l2b")
    em = st.em
    T = TT
    hba = st.inp("hba", (8, 128, T), BF16)
    pwq = st.inp("pwq", (1024, 128))
    pwk = st.inp("pwk", (1024, 128))
    pwv = st.inp("pwv", (1024, 256))
    pws = st.inp("pws", (1024, 128))
    pwau = st.inp("pwau", (1024, 64))
    pwad = st.inp("pwad", (64, 128))
    pwbeta = st.inp("pwbeta", (1024, 2))
    bau = st.inp("bau", (64, 1))
    bad = st.inp("bad", (128, 1))
    bbeta = st.inp("bbeta", (2, 1))
    asp = st.inp("asp", (128, 1))
    bsp = st.inp("bsp", (128, 1))
    whn = st.inp("whn", (2, 128))
    gs64b = st.inp("gs64b", (128, 128), BF16)
    gs64f = st.inp("gs64f", (128, 128))
    selB = st.inp("selB", (2, 128))
    trils = st.inp("trils", (128, 64))
    trius = st.inp("trius", (128, 64))
    triui = st.inp("triui", (128, 64))
    idst = st.inp("idst", (128, 64), BF16)
    identb = st.inp("identb", (128, 128), BF16)
    o_y = st.out("o_y", (2, 128, T), BF16)

    Q = st.sb("Q", [128, T], BF16)
    K = st.sb("K", [128, T], BF16)
    KAP = st.sb("KAP", [128, T], BF16)
    SDP = st.sb("SDP", [128, L2WARM + T])
    SPK = st.sb("SPKB", [128, T], BF16)
    V = st.sb("V", [128, 2 * 64 * 128], BF16)
    BB = st.sb("BB", [2, T])
    SCA = st.sb("SCA", [128, 512])
    SCB = st.sb("SCB", [128, 512])
    SCC = st.sb("SCC", [128, 512])
    SCD = st.sb("SCD", [128, 128])
    ZR512 = st.sb("ZR512", [128, 512])
    HT = st.sb("HT", [128, 8 * 512], BF16)
    VS = st.sb("VS", [128, 512], BF16)
    AUS = st.sb("AUS", [64, 512], BF16)
    ACOL = st.sb("ACOL", [128, 128])
    BETAT = st.sb("BETAT", [128, 128])
    BETHI = st.sb("BETHI", [64, 128])
    S0 = st.sb("S0", [128, 128])
    S0B = st.sb("S0B", [128, 128], BF16)
    MS = st.sb("MS", [128, 32])
    HSC = st.sb("HSC", [128, 16])
    TRASH = st.sb("TRASH", [128, 16], BF16)
    XP = st.sb("XP", [128, 16 * 64], BF16)
    PMT = st.sb("PMT", [128, 4 * 128], BF16)
    MT = st.sb("MTb", [128, 4 * 128], BF16)
    YNT = st.sb("YNT", [128, 4 * 128], BF16)
    YOUT = st.sb("YOUT", [128, 2 * 64], BF16)
    KCT = st.sb("KCT", [128, 2 * 64], BF16)
    WQT = st.sb("WQT", [128, 1024], BF16)
    WKT = st.sb("WKT", [128, 1024], BF16)
    WVT = st.sb("WVT", [128, 2048], BF16)
    WST = st.sb("WST", [128, 1024], BF16)
    WAU = st.sb("WAU", [128, 512], BF16)
    WAD = st.sb("WAD", [64, 128], BF16)
    WBE = st.sb("WBE", [128, 16], BF16)
    BAUs = st.sb("BAUs", [64, 1])
    BADs = st.sb("BADs", [128, 1])
    BBEs = st.sb("BBEs", [2, 1])
    ASPs = st.sb("ASPs", [128, 1])
    BSPs = st.sb("BSPs", [128, 1])
    WHNs = st.sb("WHNs", [128, 256])
    GS64B = st.sb("GS64B", [128, 128], BF16)
    GS64F = st.sb("GS64F", [128, 128])
    SELBs = st.sb("SELBs", [2, 128])
    TRLs = st.sb("TRLs", [128, 64])
    TRUs = st.sb("TRUs", [128, 64])
    TRIs = st.sb("TRIs", [128, 64])
    IDST = st.sb("IDST", [128, 64], BF16)
    IDB = st.sb("IDB", [128, 128], BF16)
    PS = [st.ps(f"PS{i}", [128, 512]) for i in range(6)]
    PSB = st.ps("PSBb", [128, 512], BF16)

    st.em.add("gpsimd", lambda e: e.memset(ZR512[:, :], 0.0), writes=("ZR512",))
    for t, ap, k in ((BAUs, bau, "bau"), (BADs, bad, "bad"), (BBEs, bbeta, "bbe"),
                     (ASPs, asp, "asp"), (BSPs, bsp, "bsp"), (SELBs, selB, "selb"),
                     (TRLs, trils, "trl"), (TRUs, trius, "tru"),
                     (TRIs, triui, "tri"), (IDST, idst, "idst"),
                     (IDB, identb, "idb"),
                     (GS64B, gs64b, "gs64"), (GS64F, gs64f, "gs64f")):
        em.dma("sync", t[:, :], ap, writes=(k,))
    for g in range(2):
        em.dma("sync", WHNs[:, g * 128:(g + 1) * 128],
               whn[g:g + 1, :].broadcast_to([128, 128]), writes=("whn",))
    for wsb, wdr, cols, kk in ((WQT, pwq, 128, "wq"), (WKT, pwk, 128, "wk"),
                               (WVT, pwv, 256, "wv"), (WST, pws, 128, "ws"),
                               (WAU, pwau, 64, "wau"), (WBE, pwbeta, 2, "wbe")):
        for kc in range(8):
            em.dma("gpsimd", wsb[:, kc * cols:(kc + 1) * cols],
                   wdr[kc * 128:(kc + 1) * 128, :], writes=(kk,))
    em.dma("gpsimd", WAD[:, :], pwad, writes=("wad",))

    def projmm(e, wsb, cols, coff, toff, w, ps):
        inst = None
        for kc in range(8):
            inst = e.matmul(ps[0:min(cols, 128) if cols <= 128 else 128, 0:w],
                            wsb[:, kc * cols + coff: kc * cols + coff + min(cols - coff, 128)],
                            HT[:, kc * 512:kc * 512 + w],
                            start=(kc == 0), stop=(kc == 7))
        return inst

    nt = T // 512
    for tc in range(nt):
        cs = slice(tc * 512, (tc + 1) * 512)
        for dc in range(8):
            em.dma("sync", HT[:, dc * 512:(dc + 1) * 512], hba[dc][:, cs],
                   writes=("HT",))
        hkeys = ("HT",)
        em.add("tensor", lambda e: projmm(e, WQT, 128, 0, 0, 512, PS[0]),
               reads=hkeys + ("wq",), writes=("PS0",))
        e_copy(st, Q[:, cs], PS[0][:, 0:512], reads=("PS0",), writes=("Qb",),
               eng="scalar")
        em.add("tensor", lambda e: projmm(e, WKT, 128, 0, 0, 512, PS[1]),
               reads=hkeys + ("wk",), writes=("PS1",))
        e_copy(st, K[:, cs], PS[1][:, 0:512], reads=("PS1",), writes=("Kb",),
               eng="scalar")
        em.add("tensor", lambda e: projmm(e, WST, 128, 0, 0, 512, PS[2]),
               reads=hkeys + ("ws",), writes=("PS2",))
        e_copy(st, SDP[:, L2WARM + tc * 512:L2WARM + (tc + 1) * 512],
               PS[2][:, 0:512], reads=("PS2",), writes=("SDb",), eng="scalar")
        for g in range(2):
            em.add("tensor", lambda e, g=g: projmm(e, WVT, 256, g * 128, 0, 512, PS[3]),
                   reads=hkeys + ("wv",), writes=("PS3",))
            e_copy(st, VS[:, :], PS[3][:, 0:512], reads=("PS3",), writes=("VSb",),
                   eng="scalar")
            for j in range(4):
                em.add("tensor",
                       lambda e, j=j: e.transpose(
                           PSB[:, 0:128], VS[:, j * 128:(j + 1) * 128], IDB[:, :]),
                       reads=("VSb", "idb"), writes=("PSB",))
                blk = tc * 4 + j
                e_copy(st, V[:, g * 8192 + blk * 128:g * 8192 + (blk + 1) * 128],
                       PSB[:, 0:128], reads=("PSB",), writes=("Vb",))
        em.add("tensor", lambda e: projmm(e, WAU, 64, 0, 0, 512, PS[4]),
               reads=hkeys + ("wau",), writes=("PS4",))
        e_act(st, AUS[:, :], PS[4][0:64, 0:512], AF.Silu, bias=BAUs[:, 0:1],
              reads=("PS4", "bau"), writes=("AUSb",))
        em.add("tensor",
               lambda e: e.matmul(PS[5][:, 0:512], WAD[:, :], AUS[:, :],
                                  start=True, stop=True),
               reads=("AUSb", "wad"), writes=("PS5",))
        e_ts(st, KAP[:, cs], PS[5][:, 0:512], BADs[:, 0:1], None, ALU.add,
             reads=("PS5", "bad"), writes=("ABb",))
        def bbmm(e):
            inst = None
            for kc in range(8):
                inst = e.matmul(PS[4][0:2, 0:512], WBE[:, kc * 2:(kc + 1) * 2],
                                HT[:, kc * 512:kc * 512 + 512],
                                start=(kc == 0), stop=(kc == 7))
            return inst
        em.add("tensor", bbmm, reads=hkeys + ("wbe",), writes=("PS4",))
        e_ts(st, BB[:, cs], PS[4][0:2, 0:512], BBEs[:, 0:1], None, ALU.add,
             reads=("PS4", "bbe"), writes=("BBb",))

    if PHASE == 1:
        em.dma("sync", o_y[0], Q[:, :], reads=("Qb",))
        em.dma("sync", o_y[1], K[:, :], reads=("Kb",))
        return st
    for tc in range(nt):
        cs = slice(tc * 512, (tc + 1) * 512)
        for X, kx in ((Q, "Qb"), (K, "Kb")):
            e_act(st, SCA[:, :], X[:, cs], AF.Square, reads=(kx,), writes=("SCA",))
            em.add("tensor",
                   lambda e: e.matmul(PS[0][:, 0:512], GS64F[:, :], SCA[:, :],
                                      start=True, stop=True),
                   reads=("SCA", "gs64f"), writes=("PS0",))
            e_act(st, SCB[:, :], PS[0][:, 0:512], AF.Sqrt, reads=("PS0",),
                  writes=("SCB",))
            e_ts(st, SCB[:, :], SCB[:, :], 1e-6, None, ALU.add, reads=("SCB",),
                 writes=("SCB",))
            st.em.add("vector", lambda e: e.reciprocal(SCB[:, :], SCB[:, :]),
                      reads=("SCB",), writes=("SCB",))
            e_tt(st, X[:, cs], X[:, cs], SCB[:, :], ALU.mult, reads=(kx, "SCB"),
                 writes=(kx,))

    if PHASE == 2:
        em.dma("sync", o_y[0], Q[:, :], reads=("Qb",))
        em.dma("sync", o_y[1], K[:, :], reads=("Kb",))
        return st
    nseg = T // L2SEG
    st.em.add("gpsimd", lambda e: e.memset(SDP[:, 0:L2WARM], 0.0), writes=("SDz",))
    st.em.add("gpsimd", lambda e: e.memset(MS[:, :], 0.0), writes=("M", "Mp"))
    sdv = SDP[:, :].rearrange("q (s u) -> q s u", u=L2SEG)
    spkv = SPK[:, :].rearrange("q (s u) -> q s u", u=L2SEG)
    for t in range(L2WARM + L2SEG):
        sdap = sdv[:, t // L2SEG: t // L2SEG + nseg, t % L2SEG: t % L2SEG + 1].squeeze(2)
        e_stt(st, MS[:, nseg:2 * nseg], MS[:, 0:nseg], 0.9, sdap,
              ALU.mult, ALU.add, reads=("M", "SDb", "SDz"), writes=("Mp",))
        if t < L2WARM:
            spkap, spkkey = TRASH[:, :], "TRASH"
        else:
            tau = t - L2WARM
            spkap, spkkey = spkv[:, 0:nseg, tau:tau + 1].squeeze(2), "SPKs"
        e_ts(st, spkap, MS[:, nseg:2 * nseg], 0.5, None, ALU.is_gt,
             reads=("Mp",), writes=(spkkey,))
        e_ts(st, HSC[:, :], MS[:, nseg:2 * nseg], 0.5, 0.5, ALU.is_gt, ALU.mult,
             reads=("Mp",), writes=("HSC",))
        e_tt(st, MS[:, 0:nseg], MS[:, nseg:2 * nseg], HSC[:, :], ALU.subtract,
             reads=("Mp", "HSC"), writes=("M",))

    if PHASE == 3:
        em.dma("sync", o_y[0], SPK[:, :], reads=("SPKs",))
        em.dma("sync", o_y[1], K[:, :], reads=("Kb",))
        return st
    zoff = None
    for tc in range(nt):
        cs = slice(tc * 512, (tc + 1) * 512)
        e_ts(st, HT[:, 0:512], SPK[:, cs], BSPs[:, 0:1], None, ALU.mult,
             reads=("SPKs", "bsp"), writes=("HTS",))
        def bmm(e):
            e.matmul(PS[1][:, 0:512], GS64B[:, :], HT[:, 0:512],
                     start=True, stop=False)
            return e.matmul(PS[1][:, 0:512], SELBs[:, :], BB[0:2, cs],
                            start=False, stop=True)
        em.add("tensor", bmm, reads=("HTS", "gs64", "selb", "BBb"),
               writes=("PS1",))
        em.add("tensor",
               lambda e: e.matmul(PS[0][:, 0:512], GS64B[:, :], SPK[:, cs],
                                  start=True, stop=True),
               reads=("SPKs", "gs64"), writes=("PS0",))
        e_ts(st, SCB[:, :], PS[0][:, 0:512], 0.0, None, ALU.is_gt,
             reads=("PS0",), writes=("SCB",))
        e_stt(st, SCA[:, :], SPK[:, cs], ASPs[:, 0:1], KAP[:, cs],
              ALU.mult, ALU.add, reads=("SPKs", "asp", "ABb"), writes=("SCA",))
        e_act(st, SCA[:, :], SCA[:, :], AF.Sigmoid, reads=("SCA",), writes=("SCA",))
        e_ts(st, SCA[:, :], SCA[:, :], 1.0, None, ALU.subtract, reads=("SCA",),
             writes=("SCA",))
        e_tt(st, SCA[:, :], SCA[:, :], SCB[:, :], ALU.mult, reads=("SCA", "SCB"),
             writes=("SCA",))
        e_act(st, SCC[:, :], PS[1][:, 0:512], AF.Sigmoid, reads=("PS1",),
              writes=("SCC",))
        e_tt(st, SPK[:, cs], SCC[:, :], SCB[:, :], ALU.mult,
             reads=("SCC", "SCB"), writes=("BETAb",))
        e_act(st, SCA[:, :], SCA[:, :], AF.Ln, bias=st.const(1.0),
              reads=("SCA", "consts"), writes=("SCA",))
        st.em.add("vector",
                  lambda e: e.tensor_tensor_scan(SCB[:, :], SCA[:, :],
                                                 ZR512[:, :], 0.0, ALU.add,
                                                 ALU.add),
                  reads=("SCA", "ZR512"), writes=("SCB",))
        st.em.add("gpsimd", lambda e: e.memset(SCC[:, 0:1], 0.0), writes=("SCC",))
        e_copy(st, SCC[:, 1:8],
               SCB[:, :].rearrange("q (b u) -> q b u", u=64)[:, 0:7, 63:64].squeeze(2),
               reads=("SCB",), writes=("SCC",))
        e_tt(st, SCA[:, :].rearrange("q (b u) -> q b u", u=64),
             SCB[:, :].rearrange("q (b u) -> q b u", u=64),
             SCC[:, 0:8].unsqueeze(2).broadcast_to([128, 8, 64]), ALU.subtract,
             reads=("SCB", "SCC"), writes=("SCA",))
        e_act(st, SCB[:, :], SCA[:, :], AF.Exp, reads=("SCA",), writes=("SCB",))
        e_act(st, SCC[:, :], SCA[:, :], AF.Exp, scale=st.const(-1.0),
              reads=("SCA", "consts"), writes=("SCC",))
        e_copy(st, ACOL[:, tc * 8:(tc + 1) * 8],
               SCB[:, :].rearrange("q (b u) -> q b u", u=64)[:, :, 63:64].squeeze(2),
               reads=("SCB",), writes=("ACOL",))
        e_tt(st, KAP[:, cs], K[:, cs], SCB[:, :], ALU.mult,
             reads=("Kb", "SCB"), writes=("ABb",))
        e_tt(st, Q[:, cs], Q[:, cs], SCB[:, :], ALU.mult, reads=("Qb", "SCB"),
             writes=("Qb",))
        e_tt(st, K[:, cs], K[:, cs], SCC[:, :], ALU.mult, reads=("Kb", "SCC"),
             writes=("Kb",))

    if PHASE == 4:
        em.dma("sync", o_y[0], KAP[:, :], reads=("ABb",))
        em.dma("sync", o_y[1], K[:, :], reads=("Kb",))
        return st
    for blk in range(64):
        em.add("tensor",
               lambda e, blk=blk: e.transpose(
                   PSB[:, 0:128], SPK[:, blk * 128:(blk + 1) * 128], IDB[:, :]),
               reads=("BETAb", "idb"), writes=("PSB",))
        e_copy(st, BETAT[:, blk * 2:blk * 2 + 1], PSB[:, 0:1], reads=("PSB",),
               writes=("BETATb",))
        e_copy(st, BETAT[:, blk * 2 + 1:blk * 2 + 2], PSB[:, 64:65],
               reads=("PSB",), writes=("BETATb",))

    VODD = SDP[:, :].bitcast(BF16)[0:64, 0:16384]
    em.dma("gpsimd", VODD, V[64:128, :],
           reads=("Vb",), writes=("VODD", "SDb", "SDz"))
    em.dma("gpsimd", BETAT[64:128, :].bitcast(F32)[0:0, 0:0] if False else
           BETHI[0:64, :], BETAT[64:128, :], reads=("BETATb",), writes=("BETHI",))
    st.em.add("gpsimd", lambda e: e.memset(S0[:, :], 0.0), writes=("S0",))
    NCH = T // 64
    NCH = min(NCH, int(os.environ.get("L2B_NCH", NCH)))
    for c in range(NCH):
        par = c % 2
        psA = PS[0 + 3 * par]
        psB = PS[1 + 3 * par]
        psC = PS[2 + 3 * par]
        kA, kB, kC = f"PS{3 * par}", f"PS{1 + 3 * par}", f"PS{2 + 3 * par}"
        for g in range(2):
            gb = 64 * g
            go = 256 * g
            t64 = slice(c * 64, (c + 1) * 64)
            kapc = KAP[gb:gb + 64, t64]
            kchc = K[gb:gb + 64, t64]
            rhoc = Q[gb:gb + 64, t64]
            if par == 0:
                Vc = V[0:64, g * 8192 + (c // 2) * 128: g * 8192 + (c // 2) * 128 + 128]
                vkey = "Vb"
                bT = BETAT[0:64, (c // 2) * 2 + g: (c // 2) * 2 + g + 1]
                bkey = "BETATb"
            else:
                Vc = VODD[:, g * 8192 + (c // 2) * 128: g * 8192 + (c // 2) * 128 + 128]
                vkey = "VODD"
                bT = BETHI[0:64, (c // 2) * 2 + g: (c // 2) * 2 + g + 1]
                bkey = "BETHI"
            xslot = lambda i, g=g, par=par: XP[0:64,
                                               ((par * 2 + g) * 4 + i) * 64:
                                               ((par * 2 + g) * 4 + i) * 64 + 64]
            trl = TRLs[0:64, :]
            tri = TRIs[0:64, :]
            ids = IDST[0:64, :]
            idsg = IDST[gb:gb + 64, :]
            mtg = MT[0:64, (par * 2 + g) * 128:(par * 2 + g) * 128 + 128]
            mtk = f"MTg{par * 2 + g}"
            pmg = PMT[0:64, (par * 2 + g) * 128:(par * 2 + g) * 128 + 64]
            yng = YNT[0:64, (par * 2 + g) * 128:(par * 2 + g) * 128 + 128]
            xk = lambda i, g=g, par=par: f"XP{(par * 2 + g) * 4 + i}"
            em.add("tensor", lambda e, kchc=kchc, kapc=kapc, psA=psA, go=go: (
                e.matmul(psA[0:64, go:go + 64], kchc, kapc, start=True, stop=True),
                e.matmul(psA[0:64, go + 64:go + 128], kapc, kchc, start=True,
                         stop=True))[-1],
                reads=("ABb", "Kb"), writes=(kA,))
            e_tt(st, xslot(0), psA[0:64, go + 64:go + 128], trl, ALU.mult,
                 reads=(kA, "trl"), writes=(xk(0),))
            e_ts(st, xslot(0), xslot(0), bT, -1.0, ALU.mult, ALU.mult,
                 reads=(xk(0), bkey), writes=(xk(0),))
            em.add("tensor",
                   lambda e, ids=ids, xslot=xslot, gb=gb: e.transpose(
                       PSB[0:64, gb:gb + 64], xslot(0), ids),
                   reads=(xk(0), "idst"), writes=("PSB",))
            e_copy(st, xslot(1), PSB[0:64, gb:gb + 64], reads=("PSB",),
                   writes=(xk(1),))
            e_copy(st, S0B[gb:gb + 64, :], S0[gb:gb + 64, :], reads=("S0",),
                   writes=("S0B",))
            em.add("tensor",
                   lambda e, kapc=kapc, gb=gb, psB=psB, go=go: e.matmul(
                       psB[0:64, go:go + 128], kapc, S0B[gb:gb + 64, :],
                       start=True, stop=True),
                   reads=("ABb", "S0B"), writes=(kB,))
            e_tt(st, mtg, Vc, psB[0:64, go:go + 128], ALU.subtract,
                 reads=(vkey, kB), writes=(mtk,))
            e_ts(st, mtg, mtg, bT, None, ALU.mult, reads=(mtk, bkey),
                 writes=(mtk,))
            cur = 0
            for it in range(6):
                em.add("tensor",
                       lambda e, cur=cur, mtg=mtg, psB=psB, go=go,
                       xslot=xslot: e.matmul(
                           psB[0:64, go + 128:go + 256], xslot(cur + 1), mtg,
                           start=True, stop=True),
                       reads=(xk(cur + 1), mtk), writes=(kB,))
                e_tt(st, mtg, mtg, psB[0:64, go + 128:go + 256], ALU.add,
                     reads=(mtk, kB), writes=(mtk,))
                if it < 5:
                    nxt = 2 - cur
                    em.add("tensor", lambda e, cur=cur, psC=psC, g=g,
                           xslot=xslot: (
                        e.matmul(psC[0:64, g * 192:g * 192 + 64],
                                 xslot(cur + 1), xslot(cur), start=True, stop=True),
                        e.matmul(psC[0:64, g * 192 + 64:g * 192 + 128],
                                 xslot(cur), xslot(cur + 1), start=True,
                                 stop=True))[-1],
                        reads=(xk(cur), xk(cur + 1)), writes=(kC,))
                    e_copy(st, xslot(nxt), psC[0:64, g * 192:g * 192 + 64],
                           reads=(kC,), writes=(xk(nxt),))
                    e_copy(st, xslot(nxt + 1),
                           psC[0:64, g * 192 + 64:g * 192 + 128],
                           reads=(kC,), writes=(xk(nxt + 1),))
                    cur = nxt
            em.add("tensor",
                   lambda e, kchc=kchc, rhoc=rhoc, psC=psC, g=g: e.matmul(
                       psC[0:64, 128 + g * 192:128 + g * 192 + 64],
                       kchc, rhoc, start=True, stop=True),
                   reads=("Kb", "Qb"), writes=(kC,))
            e_tt(st, pmg, psC[0:64, 128 + g * 192:128 + g * 192 + 64],
                 tri, ALU.mult, reads=(kC, "tri"), writes=("PMg",))
            em.add("tensor",
                   lambda e, rhoc=rhoc, gb=gb, psA=psA, go=go: e.matmul(
                       psA[0:64, go + 128:go + 256], rhoc,
                       S0B[gb:gb + 64, :], start=True, stop=True),
                   reads=("Qb", "S0B"), writes=(kA,))
            em.add("tensor",
                   lambda e, mtg=mtg, psB=psB, go=go, pmg=pmg: e.matmul(
                       psB[0:64, go + 128:go + 256], pmg, mtg,
                       start=True, stop=True),
                   reads=("PMg", mtk), writes=(kB,))
            e_copy(st, SCA[0:64, 128:256], psA[0:64, go + 128:go + 256],
                   reads=(kA,), writes=("SCA",))
            e_tt(st, SCA[0:64, 128:256], SCA[0:64, 128:256],
                 psB[0:64, go + 128:go + 256], ALU.add,
                 reads=("SCA", kB), writes=("SCA",))
            em.add("tensor",
                   lambda e, kchc=kchc, idsg=idsg, g=g: e.transpose(
                       PSB[0:64, 128 + g * 64:128 + g * 64 + 64], kchc, idsg),
                   reads=("Kb", "idst"), writes=("PSB",))
            e_copy(st, KCT[0:64, g * 64:(g + 1) * 64],
                   PSB[0:64, 128 + g * 64:128 + g * 64 + 64],
                   reads=("PSB",), writes=("KCTb",))
            em.add("tensor",
                   lambda e, g=g, mtg=mtg, psC=psC, gb=gb: e.matmul(
                       psC[gb:gb + 64, 384:512],
                       KCT[0:64, g * 64:(g + 1) * 64], mtg,
                       start=True, stop=True),
                   reads=("KCTb", mtk), writes=(kC,))
            e_tt(st, SCD[gb:gb + 64, :], S0[gb:gb + 64, :],
                 psC[gb:gb + 64, 384:512], ALU.add, reads=("S0", kC),
                 writes=("SCD",))
            e_ts(st, S0[gb:gb + 64, :], SCD[gb:gb + 64, :],
                 ACOL[gb:gb + 64, c:c + 1], None, ALU.mult,
                 reads=("SCD", "ACOL"), writes=("S0",))
            e_act(st, SCB[0:64, 0:128], SCA[0:64, 128:256], AF.Square,
                  reads=("SCA",), writes=("SCB",))
            st.em.add("vector",
                      lambda e: e.reduce_sum(SCB[0:64, 128:129],
                                             SCB[0:64, 0:128], AX.X),
                      reads=("SCB",), writes=("SCB",))
            e_ts(st, SCB[0:64, 128:129], SCB[0:64, 128:129], 1.0 / 128, 1e-6,
                 ALU.mult, ALU.add, reads=("SCB",), writes=("SCB",))
            e_act(st, SCB[0:64, 128:129], SCB[0:64, 128:129], AF.Sqrt,
                  reads=("SCB",), writes=("SCB",))
            st.em.add("vector",
                      lambda e: e.reciprocal(SCB[0:64, 128:129],
                                             SCB[0:64, 128:129]),
                      reads=("SCB",), writes=("SCB",))
            e_ts(st, yng, SCA[0:64, 128:256], SCB[0:64, 128:129], None,
                 ALU.mult, reads=("SCA", "SCB"), writes=("YNTg",))
            e_tt(st, yng, yng, WHNs[0:64, g * 128:(g + 1) * 128], ALU.mult,
                 reads=("YNTg", "whn"), writes=("YNTg",))
            em.add("tensor",
                   lambda e, yng=yng, ids=ids, g=g: e.transpose(
                       PSB[:, 256 + g * 64:256 + g * 64 + 64], yng, ids),
                   reads=("YNTg", "idst"), writes=("PSB",))
            e_copy(st, YOUT[:, g * 64:(g + 1) * 64],
                   PSB[:, 256 + g * 64:256 + g * 64 + 64],
                   reads=("PSB",), writes=("YOUTg",))
            em.dma("sync", o_y[g][:, t64], YOUT[:, g * 64:(g + 1) * 64],
                   reads=("YOUTg",))
    return st


def build_l2a():
    st = Stage("l2a")
    em = st.em
    x2h = st.inp("x2h", (8, 128, NHC))
    wni = st.inp("wni", (128, 8))
    cw2 = st.inp("cw2", (128, 32))
    cb2 = st.inp("cb2", (128, 8))
    wu1 = st.inp("wu1", (1024, 64))
    bu1 = st.inp("bu1", (64, 1))
    wu2 = st.inp("wu2", (64, 1024))
    bu2 = st.inp("bu2", (128, 8))
    ones_f = st.inp("ones_f", (128, 128))
    o_xn2 = st.out("o_xn2", (8, 128, NRC))
    o_h = st.out("o_h", (8, 128, NRC), BF16)
    o_gate = st.out("o_gate", (8, 128, NRC))

    X2 = st.sb("X2", [128, 8 * NHC])
    RMS = st.sb("RMS", [128, 8 * NHC], BF16)
    HB = st.sb("HB", [128, 8 * NRC], BF16)
    XNB = st.sb("XNB", [128, 2054], BF16)
    WU1 = st.sb("WU1", [128, 8 * 64], BF16)
    WU2 = st.sb("WU2", [64, 1024], BF16)
    AUS = st.sb("AUS", [64, 512], BF16)
    ONEF = st.sb("ONEF", [128, 128])
    MST = st.sb("MST", [128, 512])
    RST = st.sb("RST", [128, 512])
    SQ = st.sb("SQ", [128, 1024])
    WNI = st.sb("WNI", [128, 8])
    CW2 = st.sb("CW2", [128, 32])
    CB2 = st.sb("CB2", [128, 8])
    BU1 = st.sb("BU1", [64, 1])
    BU2 = st.sb("BU2", [128, 8])
    GTS = st.sb("GTS", [128, 2 * 512])
    PSA = st.ps("PSA", [128, 512])
    PSB = st.ps("PSB", [128, 512])
    PSC = st.ps("PSC", [128, 512])
    PSD = st.ps("PSD", [128, 512])

    em.dma("sync", ONEF[:, :], ones_f, writes=("onef",))
    for t, ap, k in ((WNI, wni, "wni"), (CW2, cw2, "cw2"), (CB2, cb2, "cb2"),
                     (BU1, bu1, "bu1"), (BU2, bu2, "bu2")):
        em.dma("sync", t[:, :], ap, writes=(k,))
    for kc in range(8):
        em.dma("gpsimd", WU1[:, kc * 64:(kc + 1) * 64],
               wu1[kc * 128:(kc + 1) * 128, :], writes=("wu1",))
    em.dma("gpsimd", WU2[:, :], wu2, writes=("wu2",))
    for i in range(8):
        em.dma("sync", X2[:, i * NHC:(i + 1) * NHC], x2h[i], writes=(f"X2_{i}",))

    for off, w in chunks_of(NHC, 512):
        emit_ln(st,
                [X2[:, i * NHC + off: i * NHC + off + w] for i in range(8)],
                [X2[:, i * NHC + off: i * NHC + off + w] for i in range(8)],
                w, 1024, ONEF[:, :], [SQ[:, 0:w], SQ[:, 512:512 + w]],
                PSA[:, 0:w], PSB[:, 0:w], MST[:, 0:w], RST[:, 0:w], 1e-5,
                [f"X2_{i}" for i in range(8)], [f"X2_{i}" for i in range(8)],
                ["SQ0", "SQ1"], ["PSA", "PSB"], "MST", "RST")
    for i in range(8):
        em.dma("sync", o_xn2[i],
               X2[:, i * NHC:(i + 1) * NHC].rearrange("q (a p) -> q a p", p=1027)[:, :, 3:],
               reads=(f"X2_{i}",))

    for off, w in chunks_of(NHC, 512):
        for i in range(8):
            e_act(st, SQ[:, (i % 2) * 512:(i % 2) * 512 + w],
                  X2[:, i * NHC + off: i * NHC + off + w], AF.Square,
                  reads=(f"X2_{i}",), writes=(f"SQ{i % 2}",))
            em.add("tensor",
                   lambda e, i=i, w=w: e.matmul(PSA[:, 0:w], ONEF[:, :],
                                                SQ[:, (i % 2) * 512:(i % 2) * 512 + w],
                                                start=(i == 0), stop=(i == 7)),
                   reads=(f"SQ{i % 2}", "onef"), writes=("PSA",))
        e_ts(st, MST[:, 0:w], PSA[:, 0:w], 1.0 / 1024, 1e-6, ALU.mult, ALU.add,
             reads=("PSA",), writes=("MST",))
        e_act(st, MST[:, 0:w], MST[:, 0:w], AF.Sqrt, reads=("MST",),
              writes=("MST",))
        st.em.add("vector", lambda e, w=w: e.reciprocal(MST[:, 0:w], MST[:, 0:w]),
                  reads=("MST",), writes=("MST",))
        for i in range(8):
            e_tt(st, RMS[:, i * NHC + off: i * NHC + off + w],
                 X2[:, i * NHC + off: i * NHC + off + w], MST[:, 0:w], ALU.mult,
                 reads=(f"X2_{i}", "MST"), writes=(f"RMS{i}",))
            e_ts(st, RMS[:, i * NHC + off: i * NHC + off + w],
                 RMS[:, i * NHC + off: i * NHC + off + w], WNI[:, i:i + 1],
                 None, ALU.mult, reads=(f"RMS{i}", "wni"), writes=(f"RMS{i}",))

    rch = [(a * 1027 + 3, a * 1024) for a in range(2)]
    for i in range(8):
        tile = RMS[:, i * NHC:(i + 1) * NHC]
        dst = HB[:, i * NRC:(i + 1) * NRC]
        for a, (hoff, roff) in enumerate(rch):
            d = dst[:, roff:roff + 1024]
            e_ts(st, d, tile[:, hoff - 3: hoff - 3 + 1024],
                 CW2[:, i * 4:i * 4 + 1], None, ALU.mult,
                 reads=(f"RMS{i}", "cw2"), writes=(f"HB{i}",))
            for j in (1, 2, 3):
                e_stt(st, d, tile[:, hoff - 3 + j: hoff - 3 + j + 1024],
                      CW2[:, i * 4 + j:i * 4 + j + 1], d, ALU.mult, ALU.add,
                      reads=(f"RMS{i}", "cw2", f"HB{i}"), writes=(f"HB{i}",))
        e_act(st, dst, dst, AF.Silu, bias=CB2[:, i:i + 1],
              reads=(f"HB{i}", "cb2"), writes=(f"HB{i}",))
        em.dma("sync", o_h[i], dst, reads=(f"HB{i}",))

    for a in range(2):
        for off0, w in chunks_of(1024, 512):
            hoff = a * 1027 + 3 + off0
            for i in range(8):
                e_copy(st, XNB[:, 0:w] if False else XNB[:, i * 256:i * 256 + 0] if False else XNB[:, 0:0] if False else XNB[:, 0:w],
                       X2[:, i * NHC + hoff: i * NHC + hoff + w],
                       reads=(f"X2_{i}",), writes=("XNB",))
                em.add("tensor",
                       lambda e, i=i, w=w: e.matmul(
                           PSC[0:64, 0:w], WU1[:, i * 64:(i + 1) * 64],
                           XNB[:, 0:w], start=(i == 0), stop=(i == 7)),
                       reads=("XNB", "wu1"), writes=("PSC",))
            e_act(st, AUS[:, 0:w], PSC[0:64, 0:w], AF.Silu, bias=BU1[:, 0:1],
                  reads=("PSC", "bu1"), writes=("AUS",))
            for cc in range(8):
                em.add("tensor",
                       lambda e, cc=cc, w=w: e.matmul(
                           PSD[:, 0:w], WU2[:, cc * 128:(cc + 1) * 128],
                           AUS[:, 0:w], start=True, stop=True),
                       reads=("AUS", "wu2"), writes=("PSD",))
                gts = GTS[:, (cc % 2) * 512:(cc % 2) * 512 + w]
                e_ts(st, gts, PSD[:, 0:w], BU2[:, cc:cc + 1], None, ALU.add,
                     reads=("PSD", "bu2"), writes=(f"GTS{cc % 2}",))
                e_act(st, gts, gts, AF.Sigmoid, reads=(f"GTS{cc % 2}",),
                      writes=(f"GTS{cc % 2}",))
                em.dma("sync", o_gate[cc][:, a * 1024 + off0: a * 1024 + off0 + w],
                       gts, reads=(f"GTS{cc % 2}",))
    return st



L2CAP = 1024


def build_l2m():
    st = Stage("l2m")
    em = st.em
    xn2f = st.inp("xn2f", (8, 128, NRC))
    routerT = st.inp("routerT", (1024, 8))
    w1 = st.inp("w1", (8, 1024, 4096))
    w2 = st.inp("w2", (8, 4096, 1024))
    ones_f = st.inp("ones_f", (128, 128))
    identf = st.inp("identf", (128, 128))
    identb = st.inp("identb", (128, 128), BF16)
    lstrict = st.inp("lstrict", (128, 128))
    posi = st.inp("posi", (128, 16), mybir.dt.int32)
    e1024 = st.inp("e1024", (128, 8))
    oobi = st.inp("oobi", (8, L2CAP), mybir.dt.int32)
    o_moe = st.out("o_moe", (8, 128, NRC))

    xn_tok = st.nc.dram_tensor("xn_tok2", [NRC, 1024], BF16).ap()
    eout = st.nc.dram_tensor("eout2", [8 * L2CAP, 1024], F32).ap()
    cw_d = st.nc.dram_tensor("cw_d", [NRC, 8], F32).ap()
    idxl = st.nc.dram_tensor("idxl2", [8 * L2CAP, 1], mybir.dt.int32).ap()

    XF = st.sb("XF", [128, 8 * NRC], BF16)
    AR2 = st.sb("AR2", [128, 8 * NRC], BF16)
    H1 = st.sb("H1", [128, 32 * L2CAP], BF16)
    WA = st.sb("WA", [128, 8192], BF16)
    WR = st.sb("WR", [128, 64], BF16)
    TOKB = st.sb("TOKB", [128, 1024], BF16)
    SCA = st.sb("SCA", [128, 1024])
    SCB = st.sb("SCB", [128, 1024])
    MOF = st.sb("MOF", [128, 1024])
    LGT = st.sb("LGT", [128, 16 * 8])
    PRB = st.sb("PRB", [128, 16 * 8])
    EQ = st.sb("EQ", [128, 16 * 8])
    RNK = st.sb("RNK", [128, 16 * 8])
    CNT = st.sb("CNT", [128, 16 * 8])
    EXC = st.sb("EXC", [128, 16 * 8])
    CWS = st.sb("CWS", [128, 16 * 8])
    PR2 = st.sb("PR2", [128, 8])
    OH2 = st.sb("OH2", [128, 8])
    CU8 = st.sb("CU8", [128, 8])
    IDXV = st.sb("IDXV", [128, 8])
    OVF = st.sb("OVF", [128, 8])
    Z8 = st.sb("Z8", [128, 16])
    DESTF = st.sb("DESTF", [128, 2])
    DSTI = st.sb("DSTI", [128, 32], mybir.dt.int32)
    IX = st.sb("IX", [128, 64], mybir.dt.int32)
    CWG = st.sb("CWG", [128, 8])
    ONEF = st.sb("ONEF", [128, 128])
    IDF = st.sb("IDF", [128, 128])
    IDB = st.sb("IDB", [128, 128], BF16)
    LSTR = st.sb("LSTR", [128, 128])
    POSI = st.sb("POSI", [128, 16], mybir.dt.int32)
    E1024 = st.sb("E1024", [128, 8])
    PS = [st.ps(f"PS{i}", [128, 512]) for i in range(8)]

    for t, ap, k in ((ONEF, ones_f, "onef"), (IDF, identf, "idf"),
                     (IDB, identb, "idb"), (LSTR, lstrict, "lstr"),
                     (POSI, posi, "posi"), (E1024, e1024, "e1024")):
        em.dma("sync", t[:, :], ap, writes=(k,))
    st.em.add("gpsimd", lambda e: e.memset(Z8[:, :], 0.0), writes=("z8",))
    em.dma("sync", idxl[:, 0:1], oobi.rearrange("e c -> (e c)").unsqueeze(1),
           writes=("idxl",))

    for i in range(8):
        em.dma("gpsimd", XF[:, i * NRC:(i + 1) * NRC], xn2f[i],
               writes=(f"XF{i}",))
    for pc in range(16):
        for dc in range(8):
            em.add("tensor",
                   lambda e, pc=pc, dc=dc: e.transpose(
                       PS[2][:, 0:128].bitcast(BF16)[:, 0:128],
                       XF[:, dc * NRC + pc * 128: dc * NRC + (pc + 1) * 128],
                       IDB[:, :]),
                   reads=(f"XF{dc}", "idb"), writes=("PS2",))
            e_copy(st, TOKB[:, dc * 128:(dc + 1) * 128],
                   PS[2][:, 0:128].bitcast(BF16)[:, 0:128],
                   reads=("PS2",), writes=("TOKB",), eng="scalar")
        em.dma("sync", xn_tok[pc * 128:(pc + 1) * 128, :],
               TOKB[:, 0:1024], reads=("TOKB",), writes=("xn_tok",))

    for kc in range(8):
        em.dma("gpsimd", WR[:, kc * 8:(kc + 1) * 8],
               routerT[kc * 128:(kc + 1) * 128, :], writes=("WR",))
    for mi, (off, w) in enumerate(chunks_of(NRC, 512)):
        def mmg(e, off=off, w=w):
            inst = None
            for kc in range(8):
                inst = e.matmul(PS[5][0:8, 0:w],
                                WR[:, kc * 8:(kc + 1) * 8],
                                XF[:, kc * NRC + off: kc * NRC + off + w],
                                start=(kc == 0), stop=(kc == 7))
            return inst

        em.add("tensor", mmg,
               reads=("WR",) + tuple(f"XF{k}" for k in range(8)),
               writes=("PS5",))
        e_copy(st, SCA[0:8, 0:w], PS[5][0:8, 0:w], reads=("PS5",),
               writes=("SCA",))
        for j in range(w // 128):
            pc = (off + j * 128) // 128
            em.add("tensor",
                   lambda e, j=j: e.transpose(
                       PS[6][:, 0:8], SCA[0:8, j * 128:(j + 1) * 128],
                       IDF[0:8, 0:8]),
                   reads=("SCA", "idf"), writes=("PS6",))
            e_copy(st, LGT[:, pc * 8:(pc + 1) * 8], PS[6][0:128, 0:8],
                   reads=("PS6",), writes=("LGT",), eng="scalar")

    for pc in range(16):
        lg = LGT[:, pc * 8:(pc + 1) * 8]
        pr = PRB[:, pc * 8:(pc + 1) * 8]
        eq = EQ[:, pc * 8:(pc + 1) * 8]
        cw = CWS[:, pc * 8:(pc + 1) * 8]
        st.em.add("vector", lambda e, lg=lg: e.reduce_max(DESTF[:, 0:1], lg, AX.X),
                  reads=("LGT",), writes=("DESTF",))
        e_tt(st, pr, lg, DESTF[:, 0:1].broadcast_to([128, 8]), ALU.subtract,
             reads=("LGT", "DESTF"), writes=("PRB",))
        e_act(st, pr, pr, AF.Exp, reads=("PRB",), writes=("PRB",))
        st.em.add("vector", lambda e, pr=pr: e.reduce_sum(DESTF[:, 0:1], pr, AX.X),
                  reads=("PRB",), writes=("DESTF",))
        st.em.add("vector", lambda e: e.reciprocal(DESTF[:, 0:1], DESTF[:, 0:1]),
                  reads=("DESTF",), writes=("DESTF",))
        e_tt(st, pr, pr, DESTF[:, 0:1].broadcast_to([128, 8]), ALU.mult,
             reads=("PRB", "DESTF"), writes=("PRB",))

        def tops(prsrc, prk, dst, dk):
            st.em.add("vector",
                      lambda e, prsrc=prsrc: e.reduce_max(DESTF[:, 1:2], prsrc, AX.X),
                      reads=(prk,), writes=("DESTF2",))
            e_tt(st, dst, prsrc, DESTF[:, 1:2].broadcast_to([128, 8]), ALU.is_ge,
                 reads=(prk, "DESTF2"), writes=(dk,))
            st.em.add("vector",
                      lambda e, dst=dst: e.tensor_tensor_scan(
                          CU8[:, :], dst, Z8[:, 0:8], 0.0, ALU.add, ALU.add),
                      reads=(dk, "z8"), writes=("CU8",))
            e_ts(st, CU8[:, :], CU8[:, :], 1.0, None, ALU.is_equal,
                 reads=("CU8",), writes=("CU8",))
            e_tt(st, dst, dst, CU8[:, :], ALU.mult, reads=(dk, "CU8"),
                 writes=(dk,))

        tops(pr, "PRB", eq, "EQ")
        e_stt(st, PR2[:, :], eq, -2.0, pr, ALU.mult, ALU.add,
              reads=("EQ", "PRB"), writes=("PR2",))
        tops(PR2[:, :], "PR2", OH2[:, :], "OH2")
        e_tt(st, eq, eq, OH2[:, :], ALU.add, reads=("EQ", "OH2"),
             writes=("EQ",))
        e_tt(st, cw, pr, eq, ALU.mult, reads=("PRB", "EQ"), writes=("CWS",))
        st.em.add("vector", lambda e, cw=cw: e.reduce_sum(DESTF[:, 0:1], cw, AX.X),
                  reads=("CWS",), writes=("DESTF",))
        e_ts(st, DESTF[:, 0:1], DESTF[:, 0:1], 1e-9, None, ALU.add,
             reads=("DESTF",), writes=("DESTF",))
        st.em.add("vector", lambda e: e.reciprocal(DESTF[:, 0:1], DESTF[:, 0:1]),
                  reads=("DESTF",), writes=("DESTF",))
        e_tt(st, cw, cw, DESTF[:, 0:1].broadcast_to([128, 8]), ALU.mult,
             reads=("CWS", "DESTF"), writes=("CWS",))
        em.dma("sync", cw_d[pc * 128:(pc + 1) * 128, :], cw, reads=("CWS",),
               writes=("cwd",))

    for pc in range(16):
        eq = EQ[:, pc * 8:(pc + 1) * 8]
        em.add("tensor",
               lambda e, eq=eq: e.matmul(PS[5][:, 0:8], LSTR[:, :], eq,
                                         start=True, stop=True),
               reads=("EQ", "lstr"), writes=("PS5",))
        e_copy(st, RNK[:, pc * 8:(pc + 1) * 8], PS[5][:, 0:8],
               reads=("PS5",), writes=("RNK",), eng="scalar")
        em.add("tensor",
               lambda e, eq=eq: e.matmul(PS[6][:, 0:8], ONEF[:, :], eq,
                                         start=True, stop=True),
               reads=("EQ", "onef"), writes=("PS6",))
        e_copy(st, CNT[:, pc * 8:(pc + 1) * 8], PS[6][:, 0:8],
               reads=("PS6",), writes=("CNT",), eng="scalar")
    for e_ in range(8):
        exc_v = EXC[:, :].rearrange("q (c e) -> q c e", e=8)[:, :, e_:e_ + 1].squeeze(2)
        cnt_v = CNT[:, :].rearrange("q (c e) -> q c e", e=8)[:, :, e_:e_ + 1].squeeze(2)
        st.em.add("vector",
                  lambda e, exc_v=exc_v, cnt_v=cnt_v: e.tensor_tensor_scan(
                      exc_v, cnt_v, Z8[:, 0:16], 0.0, ALU.add, ALU.add),
                  reads=("CNT", "z8"), writes=("EXC",))
    e_tt(st, EXC[:, :], EXC[:, :], CNT[:, :], ALU.subtract,
         reads=("EXC", "CNT"), writes=("EXC",))
    for pc in range(16):
        eq = EQ[:, pc * 8:(pc + 1) * 8]
        e_tt(st, IDXV[:, :], RNK[:, pc * 8:(pc + 1) * 8],
             EXC[:, pc * 8:(pc + 1) * 8],
             ALU.add, reads=("RNK", "EXC"), writes=("IDXV",))
        e_ts(st, OVF[:, :], IDXV[:, :], float(L2CAP), None, ALU.is_ge,
             reads=("IDXV",), writes=("OVF",))
        e_stt(st, IDXV[:, :], OVF[:, :], 1e6, IDXV[:, :], ALU.mult, ALU.add,
              reads=("OVF", "IDXV"), writes=("IDXV",))
        e_ts(st, OVF[:, :], eq, -1e6, 1e6, ALU.mult, ALU.add,
             reads=("EQ",), writes=("OVF",))
        e_tt(st, IDXV[:, :], IDXV[:, :], OVF[:, :], ALU.add,
             reads=("IDXV", "OVF"), writes=("IDXV",))
        e_tt(st, IDXV[:, :], IDXV[:, :], E1024[:, :], ALU.add,
             reads=("IDXV", "e1024"), writes=("IDXV",))
        for hot in range(2):
            st.em.add("vector", lambda e: e.tensor_reduce(
                DESTF[:, 0:1], IDXV[:, :], AX.X, ALU.min),
                reads=("IDXV",), writes=("DESTF",))
            dsti = DSTI[:, pc * 2 + hot: pc * 2 + hot + 1]
            e_copy(st, dsti, DESTF[:, 0:1], reads=("DESTF",),
                   writes=("DSTI",))
            st.em.add("gpsimd",
                      lambda e, pc=pc, dsti=dsti: e.indirect_dma_start(
                          idxl, IOA(dsti, 0), POSI[:, pc:pc + 1], None),
                      reads=("DSTI", "posi", "idxl"), writes=("idxl",), dma=True)
            if hot == 0:
                e_tt(st, OVF[:, :], IDXV[:, :],
                     DESTF[:, 0:1].broadcast_to([128, 8]), ALU.is_equal,
                     reads=("IDXV", "DESTF"), writes=("OVF",))
                e_stt(st, IDXV[:, :], OVF[:, :], 1e6, IDXV[:, :],
                      ALU.mult, ALU.add,
                      reads=("OVF", "IDXV"), writes=("IDXV",))

    for j in range(64):
        em.dma("sync", IX[:, j:j + 1], idxl[j * 128:(j + 1) * 128, 0:1],
               reads=("idxl",), writes=("IX",))
    for ex in range(8):
        for cc in range(8):
            st.em.add("gpsimd",
                      lambda e, ex=ex, cc=cc: e.indirect_dma_start(
                          SCA[:, 0:1024].bitcast(BF16)[:, 0:1024], None,
                          xn_tok, IOA(IX[:, ex * 8 + cc: ex * 8 + cc + 1], 0)),
                      reads=("IX", "xn_tok", "SCA"), writes=("SCA",), dma=True)
            for dc in range(8):
                em.add("tensor",
                       lambda e, dc=dc: e.transpose(
                           PS[2][:, 0:128].bitcast(BF16)[:, 0:128],
                           SCA[:, 0:1024].bitcast(BF16)[:, dc * 128:(dc + 1) * 128],
                           IDB[:, :]),
                       reads=("SCA", "idb"), writes=("PS2",))
                e_copy(st, AR2[:, dc * 1024 + cc * 128: dc * 1024 + (cc + 1) * 128],
                       PS[2][:, 0:128].bitcast(BF16)[:, 0:128],
                       reads=("PS2",), writes=(f"XGF{dc}",), eng="scalar")
        for g in range(8):
            gs = g % 2
            wk = "WAa" if gs == 0 else "WAb"
            for kc in range(8):
                em.dma("gpsimd",
                       WA[:, gs * 4096 + kc * 512:gs * 4096 + (kc + 1) * 512],
                       w1[ex][kc * 128:(kc + 1) * 128, g * 512:(g + 1) * 512],
                       writes=(wk,))
            for cc4 in range(4):
                hc = g * 4 + cc4
                for half in range(2):
                    ps = PS[2 + half]
                    psk = f"PS{2 + half}"

                    def mmfn(e, cc4=cc4, gs=gs, half=half, ps=ps):
                        inst = None
                        for kc in range(8):
                            inst = e.matmul(
                                ps[:, 0:512],
                                WA[:, gs * 4096 + kc * 512 + cc4 * 128:
                                    gs * 4096 + kc * 512 + (cc4 + 1) * 128],
                                AR2[:, kc * 1024 + half * 512:
                                    kc * 1024 + half * 512 + 512],
                                start=(kc == 0), stop=(kc == 7))
                        return inst

                    em.add("tensor", mmfn,
                           reads=(wk,) + tuple(f"XGF{k}" for k in range(8)),
                           writes=(psk,))
                    e_act(st, H1[:, hc * 1024 + half * 512: hc * 1024 + half * 512 + 512],
                          ps[:, 0:512], AF.Gelu, reads=(psk,),
                          writes=(f"H1_{hc}",))
        for half in range(2):
            for q in range(4):
                for kk in range(8):
                    kc = q * 8 + kk
                    wk = "WAa" if kk < 4 else "WAb"
                    em.dma("gpsimd", WA[:, kk * 1024:(kk + 1) * 1024],
                           w2[ex][kc * 128:(kc + 1) * 128, :], writes=(wk,))
                for dcc in range(8):
                    def mmfn(e, q=q, dcc=dcc, half=half):
                        inst = None
                        for kk in range(8):
                            kc = q * 8 + kk
                            inst = e.matmul(
                                PS[dcc][:, 0:512],
                                WA[:, kk * 1024 + dcc * 128: kk * 1024 + (dcc + 1) * 128],
                                H1[:, kc * 1024 + half * 512: kc * 1024 + half * 512 + 512],
                                start=(q == 0 and kk == 0),
                                stop=(q == 3 and kk == 7))
                        return inst

                    em.add("tensor", mmfn,
                           reads=("WAa", "WAb") + tuple(
                               f"H1_{q * 8 + kk}" for kk in range(8)),
                           writes=(f"PS{dcc}",))
            OUTF = AR2[:, 8192:16384].bitcast(F32)
            for dcc in range(8):
                e_copy(st, OUTF[:, dcc * 512:(dcc + 1) * 512], PS[dcc][:, 0:512],
                       reads=(f"PS{dcc}",), writes=(f"OUTF{dcc}",))
            for cc4 in range(4):
                cc = half * 4 + cc4
                st.em.add("gpsimd",
                          lambda e, ex=ex, cc=cc: e.indirect_dma_start(
                              CWG[:, :], None, cw_d,
                              IOA(IX[:, ex * 8 + cc: ex * 8 + cc + 1], 0)),
                          reads=("IX", "cwd"), writes=("CWG",), dma=True)
                for dc in range(8):
                    em.add("tensor",
                           lambda e, dc=dc, cc4=cc4, OUTF=OUTF: e.transpose(
                               PS[2][:, 0:128],
                               OUTF[:, dc * 512 + cc4 * 128: dc * 512 + (cc4 + 1) * 128],
                               IDF[:, :]),
                           reads=(f"OUTF{dc}", "idf"), writes=("PS2",))
                    e_copy(st, SCB[:, dc * 128:(dc + 1) * 128],
                           PS[2][:, 0:128], reads=("PS2",), writes=("SCB",),
                           eng="scalar")
                e_ts(st, SCB[:, 0:1024], SCB[:, 0:1024],
                     CWG[:, ex:ex + 1], None, ALU.mult,
                     reads=("SCB", "CWG"), writes=("SCB",))
                em.dma("sync",
                       eout[ex * L2CAP + cc * 128: ex * L2CAP + (cc + 1) * 128, :],
                       SCB[:, 0:1024], reads=("SCB",), writes=("eout",))

    for pc in range(16):
        st.em.add("gpsimd", lambda e: e.memset(SCA[:, :], 0.0),
                  reads=("SCA",), writes=("SCA",))
        st.em.add("gpsimd", lambda e: e.memset(SCB[:, :], 0.0),
                  reads=("SCB",), writes=("SCB",))
        for hot, (buf, bk) in enumerate(((SCA, "SCA"), (SCB, "SCB"))):
            st.em.add("gpsimd",
                      lambda e, pc=pc, hot=hot, buf=buf: e.indirect_dma_start(
                          buf[:, :], None, eout,
                          IOA(DSTI[:, pc * 2 + hot: pc * 2 + hot + 1], 0)),
                      reads=("DSTI", "eout", bk), writes=(bk,), dma=True)
        e_tt(st, SCB[:, :], SCB[:, :], SCA[:, :], ALU.add,
             reads=("SCA", "SCB"), writes=("SCB",))
        for dc in range(8):
            em.add("tensor",
                   lambda e, dc=dc: e.transpose(
                       PS[2][:, 0:128], SCB[:, dc * 128:(dc + 1) * 128],
                       IDF[:, :]),
                   reads=("SCB", "idf"), writes=("PS2",))
            e_copy(st, MOF[:, dc * 128:(dc + 1) * 128], PS[2][:, 0:128],
                   reads=("PS2",), writes=("MOF",))
        for dc in range(8):
            em.dma("sync", o_moe[dc][:, pc * 128:(pc + 1) * 128],
                   MOF[:, dc * 128:(dc + 1) * 128], reads=("MOF",))
    return st


def build_l2c():
    st = Stage("l2c")
    em = st.em
    x2f = st.inp("x2f", (8, 128, NRC))
    xn2f = st.inp("xn2f", (8, 128, NRC))
    gatef = st.inp("gatef", (8, 128, NRC))
    moef = st.inp("moef", (8, 128, NRC))
    ynb = st.inp("ynb", (8, 128, NRC), BF16)
    w_out = st.inp("w_out", (1024, 1024))
    wffnh = st.inp("wffnh", (128, 8))
    wff1 = st.inp("wff1", (1024, 4096))
    wff3 = st.inp("wff3", (1024, 4096))
    wff2 = st.inp("wff2", (4096, 1024))
    ones_f = st.inp("ones_f", (128, 128))
    o_x3 = st.out("o_x3", (8, 128, NRC))
    o_xn3 = st.out("o_xn3", (8, 128, NRC))

    h1_d = st.nc.dram_tensor("h1_d", [32, 128, NRC], BF16).ap()

    A = st.sb("A", [128, 8 * NRC])
    H2B = st.sb("H2B", [128, 8 * NRC], BF16)
    YB = st.sb("YB", [128, 8 * 512], BF16)
    H1C = st.sb("H1C", [128, 32 * 512], BF16)
    WO = st.sb("WO", [128, 8192], BF16)
    WF1 = st.sb("WF1", [128, 1024], BF16)
    WF3 = st.sb("WF3", [128, 1024], BF16)
    FF2S = st.sb("FF2S", [128, 32 * 128], BF16)
    XIN = st.sb("XIN", [128, 1024])
    GL = [st.sb(f"GL{i}", [128, 512]) for i in range(2)]
    XL = [st.sb(f"XL{i}", [128, 512]) for i in range(2)]
    SQ = st.sb("SQ", [128, 1024])
    MST = st.sb("MST", [128, 512])
    RST = st.sb("RST", [128, 512])
    WFN = st.sb("WFN", [128, 8])
    ONEF = st.sb("ONEF", [128, 128])
    PS = [st.ps(f"PS{i}", [128, 512]) for i in range(8)]

    em.dma("sync", ONEF[:, :], ones_f, writes=("ones_f",))
    em.dma("sync", WFN[:, :], wffnh, writes=("wfn",))
    for kc in range(8):
        em.dma("gpsimd", WO[:, kc * 1024:(kc + 1) * 1024],
               w_out[kc * 128:(kc + 1) * 128, :], writes=("WO",))

    for ci, (off, w) in enumerate(chunks_of(NRC, 512)):
        for kc in range(8):
            em.dma("gpsimd", YB[:, kc * 512:kc * 512 + w],
                   ynb[kc][:, off:off + w], writes=("YB",))
        for oc in range(8):
            ps = PS[oc]
            psk = f"PS{oc}"

            def mmo(e, oc=oc, off=off, w=w, ps=ps):
                inst = None
                for kc in range(8):
                    inst = e.matmul(ps[:, 0:w],
                                    WO[:, kc * 1024 + oc * 128: kc * 1024 + (oc + 1) * 128],
                                    YB[:, kc * 512:kc * 512 + w],
                                    start=(kc == 0), stop=(kc == 7))
                return inst

            em.add("tensor", mmo, reads=("WO", "YB"), writes=(psk,))
            gl = GL[oc % 2]
            xl = XL[oc % 2]
            em.dma("sync", gl[:, 0:w], gatef[oc][:, off:off + w],
                   writes=(f"GL{oc % 2}",))
            em.dma("sync", xl[:, 0:w], xn2f[oc][:, off:off + w],
                   writes=(f"XL{oc % 2}",))
            a_t = A[:, oc * NRC + off: oc * NRC + off + w]
            e_tt(st, a_t, ps[:, 0:w], gl[:, 0:w], ALU.mult,
                 reads=(psk, f"GL{oc % 2}"), writes=(f"A{oc}",))
            e_tt(st, a_t, a_t, xl[:, 0:w], ALU.add,
                 reads=(f"A{oc}", f"XL{oc % 2}"), writes=(f"A{oc}",))

    for off, w in chunks_of(NRC, 512):
        def mmsq(e, off=off, w=w):
            inst = None
            for oc in range(8):
                pass
            return inst
        for oc in range(8):
            e_act(st, SQ[:, 0:w] if oc % 2 == 0 else SQ[:, 512:512 + w],
                  A[:, oc * NRC + off: oc * NRC + off + w], AF.Square,
                  reads=(f"A{oc}",), writes=(f"SQ{oc % 2}",))
            em.add("tensor",
                   lambda e, oc=oc, off=off, w=w: e.matmul(
                       PS[2][:, 0:w], ONEF[:, :],
                       SQ[:, 0:w] if oc % 2 == 0 else SQ[:, 512:512 + w],
                       start=(oc == 0), stop=(oc == 7)),
                   reads=(f"SQ{oc % 2}", "ones_f"), writes=("PS2",))
        e_ts(st, MST[:, 0:w], PS[2][:, 0:w], 1.0 / 1024, None, ALU.mult,
             reads=("PS2",), writes=("MST",))
        e_act(st, RST[:, 0:w], MST[:, 0:w], AF.Sqrt, bias=st.const(1e-6),
              reads=("MST", "consts"), writes=("RST",))
        st.em.add("vector", lambda e, w=w: e.reciprocal(RST[:, 0:w], RST[:, 0:w]),
                  reads=("RST",), writes=("RST",))
        for oc in range(8):
            h2t = H2B[:, oc * NRC + off: oc * NRC + off + w]
            e_tt(st, h2t, A[:, oc * NRC + off: oc * NRC + off + w], RST[:, 0:w],
                 ALU.mult, reads=(f"A{oc}", "RST"), writes=(f"H2B{oc}",))
            e_ts(st, h2t, h2t, WFN[:, oc:oc + 1], None, ALU.mult,
                 reads=(f"H2B{oc}", "wfn"), writes=(f"H2B{oc}",))

    for g in range(32):
        for kc in range(8):
            em.dma("gpsimd", WF1[:, kc * 128:(kc + 1) * 128],
                   wff1[kc * 128:(kc + 1) * 128, g * 128:(g + 1) * 128],
                   writes=("WF1",))
            em.dma("gpsimd", WF3[:, kc * 128:(kc + 1) * 128],
                   wff3[kc * 128:(kc + 1) * 128, g * 128:(g + 1) * 128],
                   writes=("WF3",))
        for ci, (off, w) in enumerate(chunks_of(NRC, 512)):
            def mmf(e, off=off, w=w, WF=WF1, ps=PS[4]):
                inst = None
                for kc in range(8):
                    inst = e.matmul(ps[:, 0:w], WF[:, kc * 128:(kc + 1) * 128],
                                    H2B[:, kc * NRC + off: kc * NRC + off + w],
                                    start=(kc == 0), stop=(kc == 7))
                return inst

            em.add("tensor", lambda e, f=mmf: f(e),
                   reads=("WF1",) + tuple(f"H2B{k}" for k in range(8)),
                   writes=("PS4",))
            em.add("tensor", lambda e, f=mmf: f(e, WF=WF3, ps=PS[5]),
                   reads=("WF3",) + tuple(f"H2B{k}" for k in range(8)),
                   writes=("PS5",))
            e_act(st, SQ[:, 0:w], PS[4][:, 0:w], AF.Silu,
                  reads=("PS4",), writes=("SQ0",))
            e_tt(st, H1C[:, 0:w], SQ[:, 0:w], PS[5][:, 0:w], ALU.mult,
                 reads=("SQ0", "PS5"), writes=("H1C",))
            em.dma("sync", h1_d[g][:, off:off + w], H1C[:, 0:w],
                   reads=("H1C",), writes=("h1d",))

    for ci, (off, w) in enumerate(chunks_of(NRC, 512)):
        for g in range(32):
            em.dma("sync", H1C[:, g * 512:g * 512 + w], h1_d[g][:, off:off + w],
                   reads=("h1d",), writes=("H1C",))
        for oc in range(8):
            for g in range(32):
                em.dma("gpsimd", FF2S[:, g * 128:(g + 1) * 128],
                       wff2[g * 128:(g + 1) * 128, oc * 128:(oc + 1) * 128],
                       writes=("FF2S",))
            def mm2(e, oc=oc, w=w):
                inst = None
                for g in range(32):
                    inst = e.matmul(PS[oc][:, 0:w], FF2S[:, g * 128:(g + 1) * 128],
                                    H1C[:, g * 512:g * 512 + w],
                                    start=(g == 0), stop=(g == 31))
                return inst

            em.add("tensor", mm2, reads=("FF2S", "H1C"), writes=(f"PS{oc}",))
            gl = GL[oc % 2]
            xl = XL[oc % 2]
            em.dma("sync", gl[:, 0:w], x2f[oc][:, off:off + w],
                   writes=(f"GL{oc % 2}",))
            em.dma("sync", xl[:, 0:w], moef[oc][:, off:off + w],
                   writes=(f"XL{oc % 2}",))
            a_t = A[:, oc * NRC + off: oc * NRC + off + w]
            xs = XIN[:, (oc % 2) * 512:(oc % 2) * 512 + w]
            xsk = f"XIN{oc % 2}"
            e_tt(st, xs, PS[oc][:, 0:w], a_t, ALU.add,
                 reads=(f"PS{oc}", f"A{oc}"), writes=(xsk,))
            e_tt(st, xs, xs, gl[:, 0:w], ALU.add,
                 reads=(xsk, f"GL{oc % 2}"), writes=(xsk,))
            e_tt(st, a_t, xs, xl[:, 0:w], ALU.add,
                 reads=(xsk, f"XL{oc % 2}"), writes=(f"A{oc}",))
            em.dma("sync", o_x3[oc][:, off:off + w], a_t, reads=(f"A{oc}",))

    H1F = H1C[:, :].bitcast(F32)
    for off, w in chunks_of(NRC, 512):
        emit_ln(st,
                [A[:, oc * NRC + off: oc * NRC + off + w] for oc in range(8)],
                [H1F[:, oc * 512:oc * 512 + w] for oc in range(8)],
                w, 1024, ONEF[:, :], [SQ[:, 0:w], SQ[:, 512:512 + w]],
                PS[2][:, 0:w], PS[3][:, 0:w], MST[:, 0:w], RST[:, 0:w], 1e-5,
                [f"A{oc}" for oc in range(8)], ["H1C"] * 8,
                ["SQ0", "SQ1"], ["PS2", "PS3"], "MST", "RST")
        for oc in range(8):
            em.dma("sync", o_xn3[oc][:, off:off + w],
                   H1F[:, oc * 512:oc * 512 + w], reads=("H1C",))
    return st


def build_l3a():
    st = Stage("l3a")
    em = st.em
    xn3f = st.inp("xn3f", (8, 128, NRC))
    wqT = st.inp("wqT", (1024, 1024))
    wqTs = st.inp("wqTs", (1024, 1024))
    wkT = st.inp("wkT", (1024, 1024))
    wkTs = st.inp("wkTs", (1024, 1024))
    wvT = st.inp("wvT", (1024, 1024))
    wbT = st.inp("wbT", (1024, 64))
    wmT = st.inp("wmT", (1024, 64))
    bb64 = st.inp("bb64", (64, 1))
    bm64 = st.inp("bm64", (64, 1))
    dmix = st.inp("dmix", (64, 64))
    ccq = st.inp("ccq", (128, NRC))
    ssq = st.inp("ssq", (128, NRC))
    o_q = st.out("o_q", (8, 128, NRC), BF16)
    o_k = st.out("o_k", (8, 128, NRC), BF16)
    o_v = st.out("o_v", (8, 128, NRC), BF16)
    o_beta = st.out("o_beta", (64, NRC))
    o_mix = st.out("o_mix", (64, NRC))

    XNB = st.sb("XNB", [128, 8 * NRC], BF16)
    WQ = st.sb("WQ", [128, 8192], BF16)
    WQS = st.sb("WQS", [128, 8192], BF16)
    WK = st.sb("WK", [128, 8192], BF16)
    WKS = st.sb("WKS", [128, 8192], BF16)
    WV = st.sb("WV", [128, 8192], BF16)
    DMX = st.sb("DMX", [128, 64])
    BB = st.sb("BB", [128, 1])
    BM = st.sb("BM", [128, 1])
    CC = st.sb("CC", [128, NRC])
    SS = st.sb("SS", [128, NRC])
    T1 = st.sb("T1", [128, 512])
    T2 = st.sb("T2", [128, 512])
    TM = st.sb("TM", [128, 512])
    TE = st.sb("TE", [128, 512])
    QO = st.sb("QO", [128, 512], BF16)
    BT = st.sb("BT", [128, 512])
    PS = [st.ps(f"PS{i}", [128, 512]) for i in range(6)]

    for i in range(8):
        em.dma("gpsimd", XNB[:, i * NRC:(i + 1) * NRC], xn3f[i],
               writes=(f"XNB{i}",))
    for buf, ap, k in ((WQ, wqT, "WQ"), (WQS, wqTs, "WQS"), (WK, wkT, "WK"),
                       (WKS, wkTs, "WKS"), (WV, wvT, "WV")):
        for kc in range(8):
            em.dma("gpsimd", buf[:, kc * 1024:(kc + 1) * 1024],
                   ap[kc * 128:(kc + 1) * 128, :], writes=(k,))
    WB2 = st.sb("WB2", [128, 512], BF16)
    WM2 = st.sb("WM2", [128, 512], BF16)
    for kc in range(8):
        em.dma("gpsimd", WB2[:, kc * 64:(kc + 1) * 64],
               wbT[kc * 128:(kc + 1) * 128, :], writes=("WB2",))
        em.dma("gpsimd", WM2[:, kc * 64:(kc + 1) * 64],
               wmT[kc * 128:(kc + 1) * 128, :], writes=("WM2",))
    em.dma("sync", DMX[0:64, 0:64], dmix, writes=("DMX",))
    em.dma("sync", BB[0:64, 0:1], bb64, writes=("BB",))
    em.dma("sync", BM[0:64, 0:1], bm64, writes=("BM",))
    em.dma("sync", CC[:, :], ccq, writes=("CC",))
    em.dma("sync", SS[:, :], ssq, writes=("SS",))

    def proj_mm(ps, psk, wbuf, wk, oc, off, w):
        def mmfn(e, ps=ps, wbuf=wbuf, oc=oc, off=off, w=w):
            inst = None
            for kc in range(8):
                inst = e.matmul(ps[:, 0:w],
                                wbuf[:, kc * 1024 + oc * 128: kc * 1024 + (oc + 1) * 128],
                                XNB[:, kc * NRC + off: kc * NRC + off + w],
                                start=(kc == 0), stop=(kc == 7))
            return inst

        em.add("tensor", mmfn,
               reads=(wk,) + tuple(f"XNB{k}" for k in range(8)), writes=(psk,))

    for oc in range(8):
        for off, w in chunks_of(NRC, 512):
            for (wa, ka, wb_, kb, outap) in (
                    (WQ, "WQ", WQS, "WQS", o_q), (WK, "WK", WKS, "WKS", o_k)):
                proj_mm(PS[0], "PS0", wa, ka, oc, off, w)
                proj_mm(PS[1], "PS1", wb_, kb, oc, off, w)
                e_tt(st, T1[:, 0:w], PS[0][:, 0:w], CC[:, off:off + w], ALU.mult,
                     reads=("PS0", "CC"), writes=("T1",))
                e_tt(st, T2[:, 0:w], PS[1][:, 0:w], SS[:, off:off + w], ALU.mult,
                     reads=("PS1", "SS"), writes=("T2",))
                e_tt(st, T1[:, 0:w], T1[:, 0:w], T2[:, 0:w], ALU.add,
                     reads=("T1", "T2"), writes=("T1",))
                e_ts(st, TM[:, 0:w], T1[:, 0:w], 0.0, None, ALU.is_gt,
                     reads=("T1",), writes=("TM",))
                e_ts(st, T2[:, 0:w], T1[:, 0:w], 0.0, None, ALU.min,
                     reads=("T1",), writes=("T2",))
                e_act(st, TE[:, 0:w], T2[:, 0:w], AF.Exp, reads=("T2",),
                      writes=("TE",))
                e_ts(st, T1[:, 0:w], T1[:, 0:w], 1.0, None, ALU.add,
                     reads=("T1",), writes=("T1",))
                e_tt(st, T1[:, 0:w], T1[:, 0:w], TE[:, 0:w], ALU.subtract,
                     reads=("T1", "TE"), writes=("T1",))
                e_tt(st, T1[:, 0:w], T1[:, 0:w], TM[:, 0:w], ALU.mult,
                     reads=("T1", "TM"), writes=("T1",))
                e_tt(st, QO[:, 0:w], T1[:, 0:w], TE[:, 0:w], ALU.add,
                     reads=("T1", "TE"), writes=("QO",))
                em.dma("sync", outap[oc][:, off:off + w], QO[:, 0:w],
                       reads=("QO",))
            proj_mm(PS[0], "PS0", WV, "WV", oc, off, w)
            e_copy(st, QO[:, 0:w], PS[0][:, 0:w], reads=("PS0",),
                   writes=("QO",), eng="scalar")
            em.dma("sync", o_v[oc][:, off:off + w], QO[:, 0:w], reads=("QO",))

    for off, w in chunks_of(NRC, 512):
        def mmb(e, off=off, w=w, wbuf=WB2):
            inst = None
            for kc in range(8):
                inst = e.matmul(PS[2][0:64, 0:w], wbuf[:, kc * 64:(kc + 1) * 64],
                                XNB[:, kc * NRC + off: kc * NRC + off + w],
                                start=(kc == 0), stop=(kc == 7))
            return inst

        em.add("tensor", lambda e, f=mmb: f(e),
               reads=("WB2",) + tuple(f"XNB{k}" for k in range(8)),
               writes=("PS2",))
        e_act(st, BT[0:64, 0:w], PS[2][0:64, 0:w], AF.Sigmoid,
              bias=BB[0:64, 0:1], reads=("PS2", "BB"), writes=("BT",))
        e_ts(st, BT[0:64, 0:w], BT[0:64, 0:w], 0.9995, None, ALU.min,
             reads=("BT",), writes=("BT",))
        e_ts(st, BT[0:64, 0:w], BT[0:64, 0:w], 0.85, None, ALU.max,
             reads=("BT",), writes=("BT",))
        em.dma("sync", o_beta[:, off:off + w], BT[0:64, 0:w], reads=("BT",))
        em.add("tensor", lambda e, f=mmb: f(e, wbuf=WM2),
               reads=("WM2",) + tuple(f"XNB{k}" for k in range(8)),
               writes=("PS2",))
        e_ts(st, T1[0:64, 0:w], PS[2][0:64, 0:w], BM[0:64, 0:1], None, ALU.add,
             reads=("PS2", "BM"), writes=("T1",))
        em.add("tensor",
               lambda e, off=off, w=w: e.matmul(
                   PS[3][0:64, 0:w], DMX[0:64, 0:64], T1[0:64, 0:w],
                   start=True, stop=True),
               reads=("DMX", "T1"), writes=("PS3",))
        e_act(st, BT[0:64, 0:w], PS[3][0:64, 0:w], AF.Sigmoid,
              reads=("PS3",), writes=("BT",))
        em.dma("sync", o_mix[:, off:off + w], BT[0:64, 0:w], reads=("BT",))
    return st


def build_l3b():
    st = Stage("l3b")
    em = st.em
    hq = st.inp("hq", (32, 8, 8192), BF16)
    hk = st.inp("hk", (32, 8, 8192), BF16)
    hv = st.inp("hv", (32, 8, 8192), BF16)
    bet = st.inp("bet", (16, 8192))
    mixi = st.inp("mixi", (16, 8192))
    identf = st.inp("identf", (128, 128))
    identb = st.inp("identb", (128, 128), BF16)
    e127f = st.inp("e127f", (128, 128))
    triuib = st.inp("triuib", (128, 128), BF16)
    o_y = st.out("o_y", (8, 32, 8192), BF16)

    QC = st.sb("QC", [128, 1024], BF16)
    KC = st.sb("KC", [128, 1024], BF16)
    VC = st.sb("VC", [128, 1024], BF16)
    BCH = st.sb("BCH", [128, 128])
    MCH = st.sb("MCH", [128, 128])
    LB = st.sb("LB", [128, 128])
    EE = st.sb("EE", [128, 128])
    Z = st.sb("Z", [128, 128])
    LAM = st.sb("LAM", [128, 128])
    LINV = st.sb("LINV", [128, 128])
    SREV = st.sb("SREV", [128, 128])
    LAMT = st.sb("LAMT", [128, 16])
    LINVT = st.sb("LINVT", [128, 16])
    SREVT = st.sb("SREVT", [128, 16])
    MIXT = st.sb("MIXT", [128, 16])
    ALC = st.sb("ALC", [128, 16])
    VA = st.sb("VA", [128, 8 * 33], BF16)
    KT = st.sb("KT", [128, 8 * 32])
    G = [st.sb(f"G{i}", [128, 128]) for i in range(2)]
    GS = [st.sb(f"GS{i}", [128, 128], BF16) for i in range(2)]
    KTS = [st.sb(f"KTS{i}", [128, 32], BF16) for i in range(2)]
    Y = [st.sb(f"Y{i}", [128, 33]) for i in range(2)]
    T33 = [st.sb(f"T33{i}", [128, 33]) for i in range(2)]
    DRM = [st.sb(f"DRM{i}", [128, 2]) for i in range(2)]
    TY = st.sb("TY", [128, 32])
    YACC = st.sb("YACC", [128, 8 * 32])
    YOB = st.sb("YOB", [128, 128], BF16)
    S0 = st.sb("S0", [128, 16 * 33])
    S0B = st.sb("S0B", [128, 16 * 33], BF16)
    IDF = st.sb("IDF", [128, 128])
    IDB = st.sb("IDB", [128, 128], BF16)
    E127 = st.sb("E127", [128, 128])
    TRIU = st.sb("TRIU", [128, 128], BF16)
    PT = st.ps("PT", [128, 512])
    PG = st.ps("PG", [128, 512])
    P2 = st.ps("P2", [128, 512])
    P3 = st.ps("P3", [128, 512])
    P4 = st.ps("P4", [128, 512])
    P5 = st.ps("P5", [128, 512])
    P6 = st.ps("P6", [128, 512])

    for t, ap, k in ((IDF, identf, "idf"), (IDB, identb, "idb"),
                     (E127, e127f, "e127"), (TRIU, triuib, "triu")):
        em.dma("sync", t[:, :], ap, writes=(k,))
    st.em.add("gpsimd", lambda e: e.memset(Z[:, :], 0.0), writes=("Z",))
    for s in range(16):
        st.em.add("gpsimd", lambda e, s=s: e.memset(S0[0:32, s * 33:(s + 1) * 33], 0.0),
                  writes=(f"S0_{s}",))
        st.em.add("gpsimd", lambda e, s=s: e.memset(S0B[0:32, s * 33:(s + 1) * 33], 0.0),
                  writes=(f"S0B{s}",))
    for j in range(8):
        st.em.add("gpsimd", lambda e, j=j: e.memset(VA[:, j * 33 + 32: j * 33 + 33], 1.0),
                  writes=(f"VA{j}",))

    for c in range(64):
        t0 = c * 128
        em.dma("sync",
               QC[0:32, 0:1024].rearrange("d (p t) -> d p t", p=8),
               hq[:, :, t0:t0 + 128], writes=("QC",))
        em.dma("sync",
               KC[0:32, 0:1024].rearrange("d (p t) -> d p t", p=8),
               hk[:, :, t0:t0 + 128], writes=("KC",))
        em.dma("sync",
               VC[0:32, 0:1024].rearrange("d (p t) -> d p t", p=8),
               hv[:, :, t0:t0 + 128], writes=("VC",))
        em.dma("gpsimd", BCH[0:16, 0:128], bet[:, t0:t0 + 128], writes=("BCH",))
        em.dma("gpsimd", MCH[0:16, 0:128], mixi[:, t0:t0 + 128], writes=("MCH",))
        e_act(st, LB[0:16, 0:128], BCH[0:16, 0:128], AF.Ln,
              reads=("BCH",), writes=("LB",))
        st.em.add("vector", lambda e: e.tensor_tensor_scan(
            EE[0:16, 0:128], LB[0:16, 0:128], Z[0:16, 0:128], 0.0,
            ALU.add, ALU.add), reads=("LB", "Z"), writes=("EE",))
        e_act(st, LAM[0:16, 0:128], EE[0:16, 0:128], AF.Exp,
              reads=("EE",), writes=("LAM",))
        e_act(st, LINV[0:16, 0:128], EE[0:16, 0:128], AF.Exp, scale=-1.0,
              reads=("EE",), writes=("LINV",))
        e_ts(st, SREV[0:16, 0:128], EE[0:16, 0:128], EE[0:16, 127:128], None,
             ALU.subtract, reads=("EE",), writes=("SREV",))
        e_act(st, SREV[0:16, 0:128], SREV[0:16, 0:128], AF.Exp, scale=-1.0,
              reads=("SREV",), writes=("SREV",))
        for src, sk, dst, dk, po in ((LAM, "LAM", LAMT, "LAMT", 0),
                                     (LINV, "LINV", LINVT, "LINVT", 16),
                                     (SREV, "SREV", SREVT, "SREVT", 32),
                                     (MCH, "MCH", MIXT, "MIXT", 48)):
            em.add("tensor",
                   lambda e, src=src, po=po: e.transpose(
                       PT[:, po:po + 16], src[0:16, 0:128], IDF[0:16, 0:16]),
                   reads=(sk, "idf"), writes=("PT",))
            e_copy(st, dst[:, 0:16], PT[:, po:po + 16], reads=("PT",),
                   writes=(dk,), eng="scalar")
        em.add("tensor", lambda e: e.matmul(P6[:, 0:16], E127[:, :],
                                            LAMT[:, 0:16], start=True, stop=True),
               reads=("e127", "LAMT"), writes=("P6",))
        e_copy(st, ALC[:, 0:16], P6[:, 0:16], reads=("P6",), writes=("ALC",))

        for j in range(8):
            jc = slice(j * 128, (j + 1) * 128)
            gj = G[j % 2]
            gjk = f"G{j % 2}"
            em.add("tensor",
                   lambda e, j=j: e.transpose(
                       PT[:, 64:128].bitcast(BF16)[:, 0:32],
                       VC[0:32, j * 128:(j + 1) * 128], IDB[0:32, 0:32]),
                   reads=("VC", "idb"), writes=("PT",))
            e_copy(st, VA[:, j * 33: j * 33 + 32],
                   PT[:, 64:128].bitcast(BF16)[:, 0:32],
                   reads=("PT",), writes=(f"VA{j}",), eng="scalar")
            em.add("tensor",
                   lambda e, j=j: e.transpose(
                       PT[:, 128:192].bitcast(BF16)[:, 0:32],
                       KC[0:32, j * 128:(j + 1) * 128], IDB[0:32, 0:32]),
                   reads=("KC", "idb"), writes=("PT",))
            e_copy(st, KT[:, j * 32:(j + 1) * 32],
                   PT[:, 128:192].bitcast(BF16)[:, 0:32],
                   reads=("PT",), writes=(f"KT{j}",))
            em.add("tensor",
                   lambda e, j=j: e.matmul(
                       PG[:, 0:128], KC[0:32, j * 128:(j + 1) * 128],
                       QC[0:32, j * 128:(j + 1) * 128], start=True, stop=True),
                   reads=("KC", "QC"), writes=("PG",))
            e_copy(st, gj[:, :], PG[:, 0:128], reads=("PG",), writes=(gjk,))
            for m in range(2):
                s = j * 2 + m
                gs = GS[s % 2]
                gsk = f"GS{s % 2}"
                kts = KTS[s % 2]
                ktsk = f"KTS{s % 2}"
                yb = Y[s % 2]
                ybk = f"Y{s % 2}"
                drm = DRM[s % 2]
                drmk = f"DRM{s % 2}"
                e_ts(st, gs[:, :], gj[:, :], LINVT[:, s:s + 1], None, ALU.mult,
                     reads=(gjk, "LINVT"), writes=(gsk,))
                e_tt(st, gs[:, :], gs[:, :], TRIU[:, :], ALU.mult,
                     reads=(gsk, "triu"), writes=(gsk,))
                em.add("tensor",
                       lambda e, gs=gs, j=j: e.matmul(
                           P2[:, 0:33], gs[:, :], VA[:, j * 33:(j + 1) * 33],
                           start=True, stop=True),
                       reads=(gsk, f"VA{j}"), writes=("P2",))
                em.add("tensor",
                       lambda e, j=j, s=s: e.matmul(
                           P3[:, 0:33], QC[0:32, j * 128:(j + 1) * 128],
                           S0B[0:32, s * 33:(s + 1) * 33],
                           start=True, stop=True),
                       reads=("QC", f"S0B{s}"), writes=("P3",))
                t33 = T33[s % 2]
                t33k = f"T33{s % 2}"
                e_ts(st, yb[:, :], P2[:, 0:33], LAMT[:, s:s + 1], None, ALU.mult,
                     reads=("P2", "LAMT"), writes=(ybk,))
                e_ts(st, t33[:, :], P3[:, 0:33], LAMT[:, s:s + 1], None, ALU.mult,
                     reads=("P3", "LAMT"), writes=(t33k,))
                e_tt(st, yb[:, :], yb[:, :], t33[:, :], ALU.add,
                     reads=(ybk, t33k), writes=(ybk,))
                e_ts(st, drm[:, 0:1], yb[:, 32:33], 1e-6, None, ALU.add,
                     reads=(ybk,), writes=(drmk,))
                st.em.add("vector",
                          lambda e, drm=drm: e.reciprocal(drm[:, 0:1], drm[:, 0:1]),
                          reads=(drmk,), writes=(drmk,))
                e_tt(st, drm[:, 1:2], drm[:, 0:1], MIXT[:, s:s + 1], ALU.mult,
                     reads=(drmk, "MIXT"), writes=(drmk,))
                ya = YACC[:, j * 32:(j + 1) * 32]
                if m == 0:
                    e_ts(st, ya, yb[:, 0:32], drm[:, 1:2], None, ALU.mult,
                         reads=(ybk, drmk), writes=(f"YA{j}",))
                else:
                    e_ts(st, TY[:, 0:32], yb[:, 0:32], drm[:, 1:2], None,
                         ALU.mult, reads=(ybk, drmk), writes=("TY",))
                    e_tt(st, ya, ya, TY[:, 0:32], ALU.add,
                         reads=(f"YA{j}", "TY"), writes=(f"YA{j}",))
                e_ts(st, kts[:, 0:32], KT[:, j * 32:(j + 1) * 32],
                     SREVT[:, s:s + 1], None, ALU.mult,
                     reads=(f"KT{j}", "SREVT"), writes=(ktsk,))
                em.add("tensor",
                       lambda e, kts=kts, j=j: e.matmul(
                           P4[0:32, 0:33], kts[:, 0:32],
                           VA[:, j * 33:(j + 1) * 33], start=True, stop=True),
                       reads=(ktsk, f"VA{j}"), writes=("P4",))
                e_ts(st, S0[0:32, s * 33:(s + 1) * 33],
                     S0[0:32, s * 33:(s + 1) * 33], ALC[0:32, s:s + 1], None,
                     ALU.mult, reads=(f"S0_{s}", "ALC"), writes=(f"S0_{s}",))
                e_tt(st, S0[0:32, s * 33:(s + 1) * 33],
                     S0[0:32, s * 33:(s + 1) * 33], P4[0:32, 0:33], ALU.add,
                     reads=(f"S0_{s}", "P4"), writes=(f"S0_{s}",))
                e_copy(st, S0B[0:32, s * 33:(s + 1) * 33],
                       S0[0:32, s * 33:(s + 1) * 33],
                       reads=(f"S0_{s}",), writes=(f"S0B{s}",))
        for j in range(8):
            em.add("tensor",
                   lambda e, j=j: e.transpose(
                       P5[0:32, 0:128], YACC[:, j * 32:(j + 1) * 32], IDF[:, :]),
                   reads=(f"YA{j}", "idf"), writes=("P5",))
            e_copy(st, YOB[0:32, 0:128], P5[0:32, 0:128], reads=("P5",),
                   writes=("YOB",))
            em.dma("sync", o_y[j][:, t0:t0 + 128], YOB[0:32, 0:128],
                   reads=("YOB",))
    return st


def build_l3c():
    st = Stage("l3c")
    em = st.em
    xn3f = st.inp("xn3f", (8, 128, NRC))
    x3f = st.inp("x3f", (8, 128, NRC))
    ytb = st.inp("ytb", (8, 128, NRC), BF16)
    woT = st.inp("woT", (1024, 1024))
    bouth = st.inp("bouth", (128, 8))
    lngh = st.inp("lngh", (128, 8))
    lnbh = st.inp("lnbh", (128, 8))
    w1 = st.inp("w1", (1024, 512))
    b1h = st.inp("b1h", (128, 4))
    w2 = st.inp("w2", (512, 16))
    b2c = st.inp("b2c", (16, 1))
    wg1 = st.inp("wg1", (1024, 512))
    bg1h = st.inp("bg1h", (128, 4))
    wg2 = st.inp("wg2", (512, 16))
    bg2c = st.inp("bg2c", (16, 1))
    esc = st.inp("esc", (16, 1024))
    ebi = st.inp("ebi", (16, 1024))
    ones_f = st.inp("ones_f", (128, 128))
    identf = st.inp("identf", (128, 128))
    o_out = st.out("o_out", (8, 128, NRC))

    XNB = st.sb("XNB", [128, 8 * NRC], BF16)
    Zb = st.sb("Zb", [128, 8 * NRC])
    YB = st.sb("YB", [128, 8 * 512], BF16)
    WOS = st.sb("WOS", [128, 8 * 128], BF16)
    W1B = st.sb("W1B", [128, 8 * 512], BF16)
    WG1B = st.sb("WG1B", [128, 8 * 512], BF16)
    W2B = st.sb("W2B", [128, 4 * 16], BF16)
    WG2B = st.sb("WG2B", [128, 4 * 16], BF16)
    HB = st.sb("HB", [128, 4 * NRC], BF16)
    HGB = st.sb("HGB", [128, 4 * NRC], BF16)
    RS = st.sb("RS", [128, NRC])
    GT = st.sb("GT", [128, NRC])
    BIAS = st.sb("BIAS", [128, 26])
    B1 = st.sb("B1", [128, 8])
    ESC = st.sb("ESC", [128, 1024])
    EBI = st.sb("EBI", [128, 1024])
    SQ = st.sb("SQ", [128, 1024])
    MST = st.sb("MST", [128, 512])
    RST = st.sb("RST", [128, 512])
    SSC = st.sb("SSC", [128, 16])
    OH = st.sb("OH", [128, 16])
    CU = st.sb("CU", [128, 16])
    Z16 = st.sb("Z16", [128, 16])
    OHT = st.sb("OHT", [128, 128])
    DST = st.sb("DST", [128, 1])
    FIN = st.sb("FIN", [128, 128])
    XL3 = [st.sb(f"XL3{i}", [128, 512]) for i in range(2)]
    XL4 = [st.sb(f"XL4{i}", [128, 128]) for i in range(2)]
    ONEF = st.sb("ONEF", [128, 128])
    IDF = st.sb("IDF", [128, 128])
    PS = [st.ps(f"PS{i}", [128, 512]) for i in range(8)]

    em.dma("sync", ONEF[:, :], ones_f, writes=("ones_f",))
    em.dma("sync", IDF[:, :], identf, writes=("idf",))
    em.dma("sync", BIAS[:, 0:8], bouth, writes=("BIAS",))
    em.dma("sync", BIAS[:, 8:16], lngh, writes=("BIAS",))
    em.dma("sync", BIAS[:, 16:24], lnbh, writes=("BIAS",))
    em.dma("sync", BIAS[0:16, 24:25], b2c, writes=("BIAS",))
    em.dma("sync", BIAS[0:16, 25:26], bg2c, writes=("BIAS",))
    em.dma("sync", B1[:, 0:4], b1h, writes=("B1",))
    em.dma("sync", B1[:, 4:8], bg1h, writes=("B1",))
    em.dma("sync", ESC[0:16, 0:1024], esc, writes=("ESC",))
    em.dma("sync", EBI[0:16, 0:1024], ebi, writes=("EBI",))
    st.em.add("gpsimd", lambda e: e.memset(Z16[:, :], 0.0), writes=("z16",))
    for i in range(8):
        em.dma("gpsimd", XNB[:, i * NRC:(i + 1) * NRC], xn3f[i],
               writes=(f"XNB{i}",))
    for kc in range(8):
        em.dma("gpsimd", W1B[:, kc * 512:(kc + 1) * 512],
               w1[kc * 128:(kc + 1) * 128, :], writes=("W1B",))
        em.dma("gpsimd", WG1B[:, kc * 512:(kc + 1) * 512],
               wg1[kc * 128:(kc + 1) * 128, :], writes=("WG1B",))
    for rc in range(4):
        em.dma("gpsimd", W2B[:, rc * 16:(rc + 1) * 16],
               w2[rc * 128:(rc + 1) * 128, :], writes=("W2B",))
        em.dma("gpsimd", WG2B[:, rc * 16:(rc + 1) * 16],
               wg2[rc * 128:(rc + 1) * 128, :], writes=("WG2B",))

    for oc in range(8):
        for kc in range(8):
            em.dma("gpsimd", WOS[:, kc * 128:(kc + 1) * 128],
                   woT[kc * 128:(kc + 1) * 128, oc * 128:(oc + 1) * 128],
                   writes=("WOS",))
        for ci, (off, w) in enumerate(chunks_of(NRC, 512)):
            for kc in range(8):
                em.dma("gpsimd", YB[:, kc * 512:kc * 512 + w],
                       ytb[kc][:, off:off + w], writes=("YBl",))

            def mmz(e, oc=oc, off=off, w=w, ci=ci):
                inst = None
                for kc in range(8):
                    inst = e.matmul(PS[ci % 2][:, 0:w],
                                    WOS[:, kc * 128:(kc + 1) * 128],
                                    YB[:, kc * 512:kc * 512 + w],
                                    start=(kc == 0), stop=(kc == 7))
                return inst

            em.add("tensor", mmz, reads=("WOS", "YBl"),
                   writes=(f"PS{ci % 2}",))
            em.dma("sync", XL3[ci % 2][:, 0:w], xn3f[oc][:, off:off + w],
                   writes=(f"XL3{ci % 2}",))
            zt = Zb[:, oc * NRC + off: oc * NRC + off + w]
            e_ts(st, zt, PS[ci % 2][:, 0:w], BIAS[:, oc:oc + 1], None, ALU.add,
                 reads=(f"PS{ci % 2}", "BIAS"), writes=(f"Z{oc}",))
            e_tt(st, zt, zt, XL3[ci % 2][:, 0:w], ALU.add,
                 reads=(f"Z{oc}", f"XL3{ci % 2}"), writes=(f"Z{oc}",))

    for rcc in range(4):
        for off, w in chunks_of(NRC, 512):
            for (wb, wk, hb, hk, act, bap) in (
                    (W1B, "W1B", HB, "HB", AF.Relu, B1[:, rcc:rcc + 1]),
                    (WG1B, "WG1B", HGB, "HGB", AF.Tanh, B1[:, 4 + rcc:5 + rcc])):
                def mmr(e, rcc=rcc, off=off, w=w, wb=wb):
                    inst = None
                    for kc in range(8):
                        inst = e.matmul(
                            PS[2][:, 0:w],
                            wb[:, kc * 512 + rcc * 128: kc * 512 + (rcc + 1) * 128],
                            XNB[:, kc * NRC + off: kc * NRC + off + w],
                            start=(kc == 0), stop=(kc == 7))
                    return inst

                em.add("tensor", mmr,
                       reads=(wk,) + tuple(f"XNB{k}" for k in range(8)),
                       writes=("PS2",))
                e_act(st, hb[:, rcc * NRC + off: rcc * NRC + off + w],
                      PS[2][:, 0:w], act, bias=bap,
                      reads=("PS2", "B1"), writes=(hk,))
    for off, w in chunks_of(NRC, 512):
        for (hb, hk, w2b, w2k, dst, dk, bcol, sig) in (
                (HB, "HB", W2B, "W2B", RS, "RS", 24, False),
                (HGB, "HGB", WG2B, "WG2B", GT, "GT", 25, True)):
            def mm2(e, off=off, w=w, hb=hb, w2b=w2b):
                inst = None
                for rc in range(4):
                    inst = e.matmul(PS[3][0:16, 0:w],
                                    w2b[:, rc * 16:(rc + 1) * 16],
                                    hb[:, rc * NRC + off: rc * NRC + off + w],
                                    start=(rc == 0), stop=(rc == 3))
                return inst

            em.add("tensor", mm2, reads=(hk, w2k), writes=("PS3",))
            if sig:
                e_act(st, dst[0:16, off:off + w], PS[3][0:16, 0:w], AF.Sigmoid,
                      bias=BIAS[0:16, bcol:bcol + 1],
                      reads=("PS3", "BIAS"), writes=(dk,))
            else:
                e_ts(st, dst[0:16, off:off + w], PS[3][0:16, 0:w],
                     BIAS[0:16, bcol:bcol + 1], None, ALU.add,
                     reads=("PS3", "BIAS"), writes=(dk,))
    e_tt(st, RS[0:16, :], RS[0:16, :], GT[0:16, :], ALU.mult,
         reads=("RS", "GT"), writes=("RS",))

    for off, w in chunks_of(NRC, 512):
        emit_ln(st,
                [Zb[:, oc * NRC + off: oc * NRC + off + w] for oc in range(8)],
                [Zb[:, oc * NRC + off: oc * NRC + off + w] for oc in range(8)],
                w, 1024, ONEF[:, :], [SQ[:, 0:w], SQ[:, 512:512 + w]],
                PS[4][:, 0:w], PS[5][:, 0:w], MST[:, 0:w], RST[:, 0:w], 1e-5,
                [f"Z{oc}" for oc in range(8)], [f"Z{oc}" for oc in range(8)],
                ["SQ0", "SQ1"], ["PS4", "PS5"], "MST", "RST",
                g_aps=[BIAS[:, 8 + oc:9 + oc] for oc in range(8)],
                b_aps=[BIAS[:, 16 + oc:17 + oc] for oc in range(8)],
                key_gb="BIAS")

    for tc in range(16):
        t0 = tc * 128
        em.add("tensor",
               lambda e, t0=t0: e.transpose(PS[6][:, 0:16], RS[0:16, t0:t0 + 128],
                                            IDF[0:16, 0:16]),
               reads=("RS", "idf"), writes=("PS6",))
        e_copy(st, SSC[:, 0:16], PS[6][:, 0:16], reads=("PS6",), writes=("SSC",))
        st.em.add("vector", lambda e: e.reduce_max(DST[:, 0:1], SSC[:, :], AX.X),
                  reads=("SSC",), writes=("DST",))
        e_tt(st, OH[:, :], SSC[:, :], DST[:, 0:1].broadcast_to([128, 16]),
             ALU.is_ge, reads=("SSC", "DST"), writes=("OH",))
        st.em.add("vector", lambda e: e.tensor_tensor_scan(
            CU[:, :], OH[:, :], Z16[:, :], 0.0, ALU.add, ALU.add),
            reads=("OH", "z16"), writes=("CU",))
        e_ts(st, CU[:, :], CU[:, :], 1.0, None, ALU.is_equal,
             reads=("CU",), writes=("CU",))
        e_tt(st, OH[:, :], OH[:, :], CU[:, :], ALU.mult,
             reads=("OH", "CU"), writes=("OH",))
        em.add("tensor",
               lambda e: e.transpose(PS[6][0:16, 16:144], OH[:, :], IDF[:, :]),
               reads=("OH", "idf"), writes=("PS6",))
        e_copy(st, OHT[0:16, 0:128], PS[6][0:16, 16:144], reads=("PS6",),
               writes=("OHT",))
        for oc in range(8):
            em.add("tensor",
                   lambda e, oc=oc: e.matmul(
                       PS[7][:, 0:128], ESC[0:16, oc * 128:(oc + 1) * 128],
                       OHT[0:16, 0:128], start=True, stop=True),
                   reads=("ESC", "OHT"), writes=("PS7",))
            em.add("tensor",
                   lambda e, oc=oc: e.matmul(
                       PS[7][:, 128:256], EBI[0:16, oc * 128:(oc + 1) * 128],
                       OHT[0:16, 0:128], start=True, stop=True),
                   reads=("EBI", "OHT"), writes=("PS7",))
            em.dma("sync", XL3[oc % 2][:, 0:128], xn3f[oc][:, t0:t0 + 128],
                   writes=(f"XL3{oc % 2}",))
            em.dma("sync", XL4[oc % 2][:, 0:128], x3f[oc][:, t0:t0 + 128],
                   writes=(f"XL4{oc % 2}",))
            e_tt(st, FIN[:, :], XL3[oc % 2][:, 0:128], PS[7][:, 0:128],
                 ALU.mult, reads=("PS7", f"XL3{oc % 2}"), writes=("FIN",))
            e_tt(st, FIN[:, :], FIN[:, :], PS[7][:, 128:256], ALU.add,
                 reads=("FIN", "PS7"), writes=("FIN",))
            e_tt(st, FIN[:, :], FIN[:, :], Zb[:, oc * NRC + t0: oc * NRC + t0 + 128],
                 ALU.add, reads=("FIN", f"Z{oc}"), writes=("FIN",))
            e_tt(st, FIN[:, :], FIN[:, :], XL4[oc % 2][:, 0:128],
                 ALU.add, reads=("FIN", f"XL4{oc % 2}"), writes=("FIN",))
            em.dma("sync", o_out[oc][:, t0:t0 + 128], FIN[:, :], reads=("FIN",))
    return st


_PROGS = {}


def _prog(name):
    if name not in _PROGS:
        st = globals()["build_" + name]()
        nc = st.finish()
        _PROGS[name] = (nc, st.const_host())
    return _PROGS[name]


def _fm_slice(arr, k, halo=False):
    D = arr.shape[2]
    if halo:
        ap = np.pad(arr, ((0, 0), (3, 0), (0, 0)))[:, 1024 * k:1024 * k + 1027, :]
        P = 1027
    else:
        ap = arr[:, 1024 * k:1024 * (k + 1), :]
        P = 1024
    return np.ascontiguousarray(
        ap.reshape(2, P, D // 128, 128).transpose(2, 3, 0, 1)).reshape(
            D // 128, 128, 2 * P)


def _asm_fm(res, key, D=1024, dt=np.float32):
    out = np.empty((2, 8192, D), dt)
    for k in range(8):
        out[:, 1024 * k:1024 * (k + 1), :] = (
            np.asarray(res[k][key]).astype(dt).reshape(D // 128, 128, 2, 1024)
            .transpose(2, 3, 0, 1).reshape(2, 1024, D))
    return out


def kernel(**d):
    from concourse.bass_utils import run_bass_kernel_spmd
    import ml_dtypes
    BF = ml_dtypes.bfloat16
    CORES = list(range(8))
    onesf = np.ones((128, 128), np.float32)
    idf = np.eye(128, dtype=np.float32)
    idb = np.eye(128, dtype=BF)
    xf = np.ascontiguousarray(np.asarray(d["L0_attn_x"], np.float32))

    nc, ch = _prog("l0a")
    shared = dict(
        wfcT=np.ascontiguousarray(d["L0_ffn_W_fc"].T),
        wprojT=np.ascontiguousarray(d["L0_ffn_W_proj"].T),
        bln=np.concatenate([d["L0_ffn_ln_w"].reshape(32, 128).T,
                            d["L0_ffn_ln_b"].reshape(32, 128).T], 1).copy(),
        ones_f=onesf, ones_b=np.ones((128, 128), BF), consts=ch)
    maps = [dict(shared, xh=_fm_slice(xf, k, halo=True).reshape(8, 128, 2054))
            for k in range(8)]
    ra = run_bass_kernel_spmd(nc, maps, core_ids=CORES).results

    nc, ch = _prog("l0b")
    cwh = d["L0_attn_conv_w"][:, 0, :].reshape(16, 128, 4).transpose(1, 0, 2).reshape(128, 64)
    shared = dict(
        in_proj_w=d["L0_attn_in_proj_w"], x_proj_w=d["L0_attn_x_proj_w"],
        dt_proj_w=d["L0_attn_dt_proj_w"],
        lg=np.concatenate([d["L0_attn_ln1_g"].reshape(8, 128).T,
                           d["L0_attn_ln1_b"].reshape(8, 128).T], 1).copy(),
        cw=np.ascontiguousarray(cwh),
        cb=np.ascontiguousarray(d["L0_attn_conv_b"].reshape(16, 128).T),
        dtb=np.ascontiguousarray(d["L0_attn_dt_proj_b"].reshape(16, 128).T),
        ones_f=onesf, consts=ch)
    maps = [dict(shared, xnh=np.asarray(ra[k]["o_xn"])) for k in range(8)]
    rb = run_bass_kernel_spmd(nc, maps, core_ids=CORES).results

    def full(res, name, dt_, nch=16):
        arr = np.empty((nch, 128, 2, 8192), dt_)
        for k in range(8):
            arr[:, :, :, 1024 * k:1024 * (k + 1)] = np.asarray(
                res[k][name]).reshape(nch, 128, 2, 1024)
        return arr

    xc_f = full(rb, "o_xc", BF)
    z_f = full(rb, "o_z", BF)
    dt_f = full(rb, "o_dt", np.float32)
    bc_f = np.empty((32, 2, 8192), np.float32)
    for k in range(8):
        bc_f[:, :, 1024 * k:1024 * (k + 1)] = np.asarray(
            rb[k]["o_bc"]).reshape(32, 2, 1024)

    nc, ch = _prog("l0c")
    maps = []
    for k in range(8):
        a, db = k // 4, k % 4
        A_sl = np.ascontiguousarray(
            d["L0_attn_A_log"][512 * db:512 * (db + 1), :]
            .reshape(4, 128, 16).transpose(1, 0, 2)).reshape(128, 64)
        maps.append(dict(
            xcs=np.ascontiguousarray(xc_f[4 * db:4 * db + 4, :, a, :]),
            dts=np.ascontiguousarray(dt_f[4 * db:4 * db + 4, :, a, :]),
            zs=np.ascontiguousarray(z_f[4 * db:4 * db + 4, :, a, :]),
            bcs=np.ascontiguousarray(bc_f[:, a, :]), aap=A_sl,
            dap=np.ascontiguousarray(
                d["L0_attn_D"][512 * db:512 * (db + 1)].reshape(4, 128).T),
            h0=np.zeros((128, 64), np.float32)))
    rc = run_bass_kernel_spmd(nc, maps, core_ids=CORES).results

    yg_f = np.empty((16, 128, 2, 8192), BF)
    for k in range(8):
        a, db = k // 4, k % 4
        yg_f[4 * db:4 * db + 4, :, a, :] = np.asarray(rc[k]["o_yg"])

    nc, ch = _prog("l0d")
    shared = dict(
        out_proj_w=d["L0_attn_out_proj_w"], ffn_w1=d["L0_attn_ffn_w1"],
        ffn_w2=d["L0_attn_ffn_w2"],
        lg2=np.concatenate([d["L0_attn_ln2_g"].reshape(8, 128).T,
                            d["L0_attn_ln2_b"].reshape(8, 128).T], 1).copy(),
        fb1=np.ascontiguousarray(d["L0_attn_ffn_b1"].reshape(32, 128).T),
        fb2=np.ascontiguousarray(d["L0_attn_ffn_b2"].reshape(8, 128).T),
        ones_f=onesf, consts=ch)
    maps = []
    for k in range(8):
        maps.append(dict(
            shared,
            ygs=np.ascontiguousarray(
                yg_f[:, :, :, 1024 * k:1024 * (k + 1)]).reshape(16, 128, 2048),
            xn8=np.ascontiguousarray(np.asarray(ra[k]["o_xn"]).reshape(
                8, 128, 2, 1027)[:, :, :, 3:]).reshape(8, 128, 2048),
            bit8=np.asarray(ra[k]["o_bit"]),
            xs8=_fm_slice(xf, k)))
    rd = run_bass_kernel_spmd(nc, maps, core_ids=CORES).results
    x1 = _asm_fm(rd, "o_x1")

    nc, ch = _prog("l1")
    posi = (np.arange(16)[None, :] * 128 + np.arange(128)[:, None]).astype(np.int32)
    shared = dict(
        ipwT=np.ascontiguousarray(d["L1_attn_in_proj_weight"].T),
        opwT=np.ascontiguousarray(d["L1_attn_out_proj_weight"].T),
        gw=d["L1_ffn_gate_W"],
        gbcol=d["L1_ffn_gate_b"].reshape(8, 1).copy(),
        w1=d["L1_ffn_W1"],
        b1h=np.ascontiguousarray(d["L1_ffn_b1"].reshape(8, 32, 128).transpose(0, 2, 1)),
        w2=d["L1_ffn_W2"],
        b2h=np.ascontiguousarray(d["L1_ffn_b2"].reshape(8, 8, 128).transpose(0, 2, 1)),
        ones_f=onesf, identf=idf, identb=idb,
        lstrict=np.triu(np.ones((128, 128), np.float32), 1),
        posi=posi,
        e512=np.tile((np.arange(8) * 512).astype(np.float32), (128, 1)),
        oobi=np.full((8, MOE_CAP), 1 << 20, np.int32),
        consts=ch)
    maps = [dict(shared, x1s=_fm_slice(x1, k)) for k in range(8)]
    r1 = run_bass_kernel_spmd(nc, maps, core_ids=CORES).results
    x2 = _asm_fm(r1, "o_x2")

    nc, ch = _prog("l2a")
    shared = dict(
        wni=np.ascontiguousarray(d["L2_attn_w_norm_in"].reshape(8, 128).T),
        cw2=np.ascontiguousarray(
            d["L2_attn_conv_w"].reshape(8, 128, 4).transpose(1, 0, 2)).reshape(128, 32),
        cb2=np.ascontiguousarray(d["L2_attn_conv_b"].reshape(8, 128).T),
        wu1=d["L2_attn_w_u1"], bu1=d["L2_attn_b_u1"].reshape(64, 1).copy(),
        wu2=d["L2_attn_w_u2"],
        bu2=np.ascontiguousarray(d["L2_attn_b_u2"].reshape(8, 128).T),
        ones_f=onesf, consts=ch)
    maps = [dict(shared, x2h=_fm_slice(x2, k, halo=True)) for k in range(8)]
    r2a = run_bass_kernel_spmd(nc, maps, core_ids=CORES).results
    xn2 = _asm_fm(r2a, "o_xn2")
    h2b = _asm_fm(r2a, "o_h", dt=BF)
    gate2 = _asm_fm(r2a, "o_gate")

    nc, ch = _prog("l2m")
    shared = dict(
        routerT=np.ascontiguousarray(d["L2_ffn_router_w"].T),
        w1=d["L2_ffn_w1"], w2=d["L2_ffn_w2"],
        ones_f=onesf, identf=idf, identb=idb,
        lstrict=np.triu(np.ones((128, 128), np.float32), 1),
        posi=posi,
        e1024=np.tile((np.arange(8) * 1024).astype(np.float32), (128, 1)),
        oobi=np.full((8, L2CAP), 1 << 20, np.int32),
        consts=ch)
    maps = [dict(shared, xn2f=_fm_slice(xn2, k)) for k in range(8)]
    r2m = run_bass_kernel_spmd(nc, maps, core_ids=CORES).results
    moe2 = _asm_fm(r2m, "o_moe")

    nc, ch = _prog("l2b")
    gs64 = np.kron(np.eye(2), np.ones((64, 64))).astype(np.float32)
    shared = dict(
        pwau=np.ascontiguousarray(d["L2_attn_w_au"]),
        bau=d["L2_attn_b_au"].reshape(64, 1).copy(),
        gs64b=gs64.astype(BF), gs64f=gs64,
        selB=np.kron(np.eye(2), np.ones((1, 64))).astype(np.float32),
        trils=np.tile(np.tril(np.ones((64, 64), np.float32), -1), (2, 1)),
        trius=np.tile(np.triu(np.ones((64, 64), np.float32), 1), (2, 1)),
        triui=np.tile(np.triu(np.ones((64, 64), np.float32), 0), (2, 1)),
        idst=np.tile(np.eye(64, dtype=BF), (2, 1)),
        identb=idb, consts=ch)
    maps = []
    for kk in range(8):
        a = kk // 4
        h0_, h1_ = (2 * kk) % 8, (2 * kk + 1) % 8
        hs = np.ascontiguousarray(
            h2b[a].reshape(8192, 8, 128).transpose(1, 2, 0))

        def wsl(wname, cols):
            W = d["L2_attn_" + wname]
            return np.ascontiguousarray(np.concatenate(
                [W[:, h0_ * cols:(h0_ + 1) * cols],
                 W[:, h1_ * cols:(h1_ + 1) * cols]], 1))

        maps.append(dict(
            shared, hba=hs,
            pwq=wsl("wq", 64), pwk=wsl("wk", 64), pwv=wsl("wv", 128),
            pws=wsl("w_spike", 64),
            pwad=np.ascontiguousarray(np.concatenate(
                [d["L2_attn_w_ad"][:, h0_ * 64:(h0_ + 1) * 64],
                 d["L2_attn_w_ad"][:, h1_ * 64:(h1_ + 1) * 64]], 1)),
            pwbeta=np.ascontiguousarray(d["L2_attn_w_beta"][:, [h0_, h1_]]),
            bad=np.concatenate(
                [d["L2_attn_b_ad"][h0_ * 64:(h0_ + 1) * 64],
                 d["L2_attn_b_ad"][h1_ * 64:(h1_ + 1) * 64]]).reshape(128, 1),
            bbeta=d["L2_attn_b_beta"][[h0_, h1_]].reshape(2, 1).copy(),
            asp=np.concatenate([d["L2_attn_alpha_spike"][0, h0_],
                                d["L2_attn_alpha_spike"][0, h1_]]).reshape(128, 1),
            bsp=np.concatenate([d["L2_attn_beta_spike"][0, h0_],
                                d["L2_attn_beta_spike"][0, h1_]]).reshape(128, 1),
            whn=np.ascontiguousarray(d["L2_attn_w_hn"][[h0_, h1_], :])))
    r2b = run_bass_kernel_spmd(nc, maps, core_ids=CORES).results
    yflat2 = np.empty((2, 8192, 1024), BF)
    for kk in range(8):
        a = kk // 4
        for gi in range(2):
            hh = (2 * kk + gi) % 8
            yflat2[a, :, hh * 128:(hh + 1) * 128] = np.asarray(
                r2b[kk]["o_y"][gi]).T

    nc, ch = _prog("l2c")
    shared = dict(
        w_out=d["L2_attn_w_out"],
        wffnh=np.ascontiguousarray(d["L2_attn_w_ffn"].reshape(8, 128).T),
        wff1=d["L2_attn_w_ff1"], wff3=d["L2_attn_w_ff3"],
        wff2=d["L2_attn_w_ff2"], ones_f=onesf, consts=ch)
    maps = []
    for k in range(8):
        maps.append(dict(
            shared, x2f=_fm_slice(x2, k), xn2f=_fm_slice(xn2, k),
            gatef=_fm_slice(gate2, k), moef=_fm_slice(moe2, k),
            ynb=_fm_slice(yflat2, k)))
    r2c = run_bass_kernel_spmd(nc, maps, core_ids=CORES).results
    x3 = _asm_fm(r2c, "o_x3")
    xn3 = _asm_fm(r2c, "o_xn3")

    nc, ch = _prog("l3a")
    sig = np.arange(1024)
    sigp = (sig // 32) * 32 + (sig % 32 + 16) % 32
    dmx = np.zeros((64, 64), np.float32)
    for h2 in range(32):
        a0, a1 = 2 * h2, 2 * h2 + 1
        dmx[a0, a0] = 1; dmx[a1, a0] = -1
        dmx[a1, a1] = 1; dmx[a0, a1] = -1
    inv_freq = 1.0 / 10000.0 ** (np.arange(16, dtype=np.float64) / 16)
    freqs = np.arange(8192, dtype=np.float64)[:, None] * inv_freq
    cosf = np.cos(freqs).astype(np.float32)
    sinf = np.sin(freqs).astype(np.float32)
    shared = dict(
        wqT=np.ascontiguousarray(d["L3_attn_Wq"].T),
        wqTs=np.ascontiguousarray(d["L3_attn_Wq"].T[:, sigp]),
        wkT=np.ascontiguousarray(d["L3_attn_Wk"].T),
        wkTs=np.ascontiguousarray(d["L3_attn_Wk"].T[:, sigp]),
        wvT=np.ascontiguousarray(d["L3_attn_Wv"].T),
        wbT=np.ascontiguousarray(d["L3_attn_Wbeta"].T),
        wmT=np.ascontiguousarray(d["L3_attn_Wmix"].T),
        bb64=d["L3_attn_bbeta"].reshape(64, 1).copy(),
        bm64=d["L3_attn_bmix"].reshape(64, 1).copy(),
        dmix=dmx, consts=ch)
    maps = []
    for kk in range(8):
        pos = slice(1024 * kk, 1024 * (kk + 1))
        cc_rows = np.empty((128, 1024), np.float32)
        ss_rows = np.empty((128, 1024), np.float32)
        for r in range(128):
            ddr = r % 32
            if ddr < 16:
                cc_rows[r] = cosf[pos, ddr]
                ss_rows[r] = -sinf[pos, ddr]
            else:
                cc_rows[r] = cosf[pos, ddr - 16]
                ss_rows[r] = sinf[pos, ddr - 16]
        maps.append(dict(shared, xn3f=_fm_slice(xn3, kk),
                         ccq=np.tile(cc_rows, (1, 2)),
                         ssq=np.tile(ss_rows, (1, 2))))
    r3a = run_bass_kernel_spmd(nc, maps, core_ids=CORES).results
    pq3 = _asm_fm(r3a, "o_q", dt=BF)
    pk3 = _asm_fm(r3a, "o_k", dt=BF)
    v3 = _asm_fm(r3a, "o_v", dt=BF)
    bet3 = np.empty((64, 2, 8192), np.float32)
    mix3 = np.empty((64, 2, 8192), np.float32)
    for kk in range(8):
        bet3[:, :, 1024 * kk:1024 * (kk + 1)] = np.asarray(
            r3a[kk]["o_beta"]).reshape(64, 2, 1024)
        mix3[:, :, 1024 * kk:1024 * (kk + 1)] = np.asarray(
            r3a[kk]["o_mix"]).reshape(64, 2, 1024)

    nc, ch = _prog("l3b")
    e127 = np.zeros((128, 128), np.float32); e127[127, :] = 1.0
    shared = dict(identf=idf, identb=idb, e127f=e127,
                  triuib=np.triu(np.ones((128, 128), dtype=BF), 0),
                  consts=ch)
    qg = pq3.reshape(2, 8192, 32, 32)
    kg = pk3.reshape(2, 8192, 32, 32)
    vg = v3.reshape(2, 8192, 32, 32)
    maps = []
    for k2 in range(8):
        hq = np.empty((32, 8, 8192), BF)
        hk = np.empty((32, 8, 8192), BF)
        hv = np.empty((32, 8, 8192), BF)
        bet = np.empty((16, 8192), np.float32)
        mixi = np.empty((16, 8192), np.float32)
        for j in range(8):
            Pp = 8 * k2 + j
            a, hh = Pp // 32, Pp % 32
            hq[:, j, :] = qg[a, :, hh, :].T
            hk[:, j, :] = kg[a, :, hh, :].T
            hv[:, j, :] = vg[a, :, hh, :].T
            for mm_ in range(2):
                bet[j * 2 + mm_] = bet3[hh * 2 + mm_, a]
                mixi[j * 2 + mm_] = mix3[hh * 2 + mm_, a]
        maps.append(dict(shared, hq=hq, hk=hk, hv=hv, bet=bet, mixi=mixi))
    r3b = run_bass_kernel_spmd(nc, maps, core_ids=CORES).results
    y3 = np.empty((2, 8192, 32, 32), BF)
    for k2 in range(8):
        for j in range(8):
            Pp = 8 * k2 + j
            a, hh = Pp // 32, Pp % 32
            y3[a, :, hh, :] = np.asarray(r3b[k2]["o_y"][j]).T
    y3flat = np.ascontiguousarray(y3.reshape(2, 8192, 1024))

    nc, ch = _prog("l3c")
    p3 = lambda n: d["L3_ffn_" + n]
    shared = dict(
        woT=np.ascontiguousarray(d["L3_attn_Wout"].T),
        bouth=np.ascontiguousarray(d["L3_attn_bout"].reshape(8, 128).T),
        lngh=np.ascontiguousarray(d["L3_attn_ln_g"].reshape(8, 128).T),
        lnbh=np.ascontiguousarray(d["L3_attn_ln_b"].reshape(8, 128).T),
        w1=p3("W1"), b1h=np.ascontiguousarray(p3("b1").reshape(4, 128).T),
        w2=p3("W2"), b2c=p3("b2").reshape(16, 1).copy(),
        wg1=p3("Wg1"),
        bg1h=np.ascontiguousarray(p3("bg1").reshape(4, 128).T),
        wg2=np.ascontiguousarray(np.tile(p3("Wg2"), (1, 16))),
        bg2c=np.tile(p3("bg2"), 16).reshape(16, 1).copy(),
        esc=p3("expert_scales"), ebi=p3("expert_biases"),
        ones_f=onesf, identf=idf, consts=ch)
    maps = []
    for kk in range(8):
        maps.append(dict(shared, xn3f=_fm_slice(xn3, kk),
                         x3f=_fm_slice(x3, kk),
                         ytb=_fm_slice(y3flat, kk)))
    r3c = run_bass_kernel_spmd(nc, maps, core_ids=CORES).results
    return _asm_fm(r3c, "o_out")
